# version 10
# baseline (speedup 1.0000x reference)
import math
import contextlib
import numpy as np
import ml_dtypes

import concourse.bass as bass
import concourse.mybir as mybir
from concourse.bass_utils import run_bass_kernel_spmd

F32 = mybir.dt.float32
BF16 = mybir.dt.bfloat16
AF = mybir.ActivationFunctionType
ALU = mybir.AluOpType
AX = mybir.AxisListType

T = 4096
D = 1024
H = 16
HD = 64
QT = 512
NT = T // QT
NG = 2
GH = H // NG
M = 64
INNER = 16
EPS = 1e-5
CMARG = math.log(1.04)

ENGINES = ("sync", "gpsimd", "scalar", "vector", "tensor")
SEMS = {
    "S": ("sync", True),
    "G": ("gpsimd", True),
    "P": ("tensor", False),
    "A": ("scalar", False),
    "V": ("vector", False),
    "K": ("gpsimd", False),
}


def _flat(waits):
    for w in waits:
        if w is None:
            continue
        if isinstance(w, list):
            yield from _flat(w)
        else:
            yield w


class Plan:
    def __init__(self, nc, ctx):
        self.nc = nc
        self.sems = {}
        self.cnt = {}
        self.ops = {e: [] for e in ENGINES}
        self.lastw = {e: {} for e in ENGINES}
        for name in SEMS:
            self.sems[name] = ctx.enter_context(nc.semaphore(name))
            self.cnt[name] = 0

    def emit(self, eng, fn, inc=None, waits=()):
        best = {}
        for s, v in _flat(waits):
            if v > best.get(s, 0):
                best[s] = v
        ws = []
        for s, v in best.items():
            if v <= self.lastw[eng].get(s, 0):
                continue
            ws.append((s, v))
            self.lastw[eng][s] = v
        tok_out = None
        inc_rec = None
        if inc:
            amt = 16 if SEMS[inc][1] else 1
            self.cnt[inc] += amt
            tok_out = (inc, self.cnt[inc])
            inc_rec = (inc, amt)
        self.ops[eng].append((tuple(ws), fn, inc_rec))
        return tok_out

    def finish(self, blk):
        sems = self.sems
        for eng_name in ENGINES:
            ops = self.ops[eng_name]

            def make_body(ops):
                def body(eng):
                    for ws, fn, inc in ops:
                        for s, v in ws:
                            eng.wait_ge(sems[s], v)
                        instr = fn(eng)
                        if inc is not None:
                            instr.then_inc(sems[inc[0]], inc[1])
                return body
            getattr(blk, eng_name)(make_body(ops))


class Buf:
    __slots__ = ("w", "r")

    def __init__(self):
        self.w = None
        self.r = []

    def wgate(self):
        return self.r + [self.w]

    def wrote(self, tok):
        if tok is not None:
            self.w = tok
            self.r = []

    def read(self, tok):
        if tok is not None:
            self.r.append(tok)


def build_consts():
    c = {}
    c["eyebf"] = np.eye(128, dtype=ml_dtypes.bfloat16)
    c["eyef"] = np.eye(128, dtype=np.float32)
    m = np.zeros((128, 4, QT), dtype=np.float32)
    for k in range(4):
        for s in range(128):
            m[s, k, :] = (np.arange(QT) >= (k * 128 + s)).astype(np.float32)
    c["masks"] = m.reshape(128, 4 * QT).astype(ml_dtypes.bfloat16)
    e = np.zeros((128, 128), dtype=np.float32)
    e[0:64, 0] = 1.0
    e[64:128, 64] = 1.0
    c["eab"] = e.astype(ml_dtypes.bfloat16)
    ep = np.zeros((16, 8, 128), dtype=np.float32)
    for ch in range(8):
        ep[2 * ch, ch, 0:64] = 1.0
        ep[2 * ch + 1, ch, 64:128] = 1.0
    c["epat"] = ep.reshape(16, 8 * 128)
    hp = np.zeros((16, 128), dtype=np.float32)
    for h in range(16):
        hp[h, h * 8:(h + 1) * 8] = 1.0
    c["hpat"] = hp
    sp = np.zeros((128, 8, 64), dtype=np.float32)
    st = np.zeros((64, 8, 128), dtype=np.float32)
    for ch in range(8):
        for p in range(128):
            mm = (128 * ch + p) // 16
            sp[p, ch, mm] = 1.0
            st[mm, ch, p] = 1.0
    c["selpat"] = sp.reshape(128, 8 * 64).astype(ml_dtypes.bfloat16)
    c["selpatT"] = st.reshape(64, 8 * 128).astype(ml_dtypes.bfloat16)
    eb = np.zeros((64, 1024), dtype=np.float32)
    for flat in range(1024):
        eb[flat // 16, flat] = 1.0
    c["ebf"] = eb.astype(ml_dtypes.bfloat16)
    return c


def build_program(debug=False):
    nc = bass.Bass("TRN2", target_bir_lowering=False)

    x_in = nc.dram_tensor("x", [T, D], F32, kind="ExternalInput").ap()
    wqkv = nc.dram_tensor("wqkv", [3 * D, D], F32, kind="ExternalInput").ap()
    wout = nc.dram_tensor("wout", [D, D], F32, kind="ExternalInput").ap()
    s_a = nc.dram_tensor("sa", [1], F32, kind="ExternalInput").ap()
    s_b = nc.dram_tensor("sb", [1], F32, kind="ExternalInput").ap()
    s_dt = nc.dram_tensor("sdt", [1], F32, kind="ExternalInput").ap()
    fwin = nc.dram_tensor("fwin", [D, M], F32, kind="ExternalInput").ap()
    fbin = nc.dram_tensor("fbin", [M], F32, kind="ExternalInput").ap()
    fw1 = nc.dram_tensor("fw1", [M * INNER], F32, kind="ExternalInput").ap()
    fb1 = nc.dram_tensor("fb1", [M * INNER], F32, kind="ExternalInput").ap()
    fw2 = nc.dram_tensor("fw2", [M * INNER], F32, kind="ExternalInput").ap()
    fb2 = nc.dram_tensor("fb2", [M], F32, kind="ExternalInput").ap()
    fwout = nc.dram_tensor("fwout", [M, D], F32, kind="ExternalInput").ap()
    fbout = nc.dram_tensor("fbout", [D], F32, kind="ExternalInput").ap()
    c_eye = nc.dram_tensor("eyebf", [128, 128], BF16, kind="ExternalInput").ap()
    c_eyef = nc.dram_tensor("eyef", [128, 128], F32, kind="ExternalInput").ap()
    c_masks = nc.dram_tensor("masks", [128, 4 * QT], BF16, kind="ExternalInput").ap()
    c_eab = nc.dram_tensor("eab", [128, 128], BF16, kind="ExternalInput").ap()
    c_epat = nc.dram_tensor("epat", [16, 8 * 128], F32, kind="ExternalInput").ap()
    c_hpat = nc.dram_tensor("hpat", [16, 128], F32, kind="ExternalInput").ap()
    c_selpat = nc.dram_tensor("selpat", [128, 8 * 64], BF16, kind="ExternalInput").ap()
    c_selpatT = nc.dram_tensor("selpatT", [64, 8 * 128], BF16, kind="ExternalInput").ap()
    c_ebf = nc.dram_tensor("ebf", [64, 1024], BF16, kind="ExternalInput").ap()

    out = nc.dram_tensor("out", [T, D], F32, kind="ExternalOutput").ap()

    skind = "ExternalOutput" if debug else "Internal"
    U16 = mybir.dt.uint16
    x1s = nc.dram_tensor("x1s", [T, D], F32, kind=skind).ap()
    y1s = nc.dram_tensor("y1s", [T, D], U16 if debug else BF16, kind=skind).ap()
    if debug:
        y1s = y1s.bitcast(BF16)
    rawTs = nc.dram_tensor("rawTs", [D, T], U16 if debug else BF16, kind=skind).ap()
    if debug:
        rawTs = rawTs.bitcast(BF16)
    Ls = nc.dram_tensor("Ls", [H, T], F32, kind=skind).ap()
    dbg1 = nc.dram_tensor("dbg1", [128, 32], F32, kind=skind).ap() if debug else None
    dbg2 = nc.dram_tensor("dbg2", [128, 1048], F32, kind=skind).ap() if debug else None
    dbg3 = nc.dram_tensor("dbg3", [128, 1664], mybir.dt.uint16, kind=skind).ap() if debug else None
    wqkv_bf = nc.dram_tensor("wqkv_bf", [3 * D, D], BF16, kind="Internal").ap()
    wout_bf = nc.dram_tensor("wout_bf", [D, D], BF16, kind="Internal").ap()
    S_hbm = nc.dram_tensor("S_hbm", [128, 512], F32, kind="Internal").ap()
    t1s = nc.dram_tensor("t1s", [T, D], BF16, kind="Internal").ap()

    with contextlib.ExitStack() as ctx:
        def sbuf(name, shape, dt, stack=None):
            return (stack or ctx).enter_context(nc.sbuf_tensor(name, shape, dt))

        wqkvT = sbuf("wqkvT", [128, 8 * 3072], BF16)
        win = sbuf("win", [128, 8 * 64], BF16)
        woutf = sbuf("woutf", [64, 1024], BF16)
        ew1 = sbuf("ew1", [64, 8 * 128], BF16)
        bdw2 = sbuf("bdw2", [128, 8 * 64], BF16)
        b1p = sbuf("b1p", [128, 8], F32)
        hbbc = sbuf("hbbc", [128, 1024], F32)
        eab_sb = sbuf("eab_sb", [128, 128], BF16)
        eye_sb = sbuf("eye_sb", [128, 128], BF16)
        ones_b = sbuf("ones_b", [1, 128], BF16)
        ones_f = sbuf("ones_f", [128, 128], F32)
        xt = sbuf("xt", [128, 4 * 1024], F32)
        x1t = sbuf("x1t", [128, 4 * 1024], F32)
        lntok = sbuf("lntok", [128, 2 * 1024], BF16)
        tft = sbuf("tft", [128, 8 * 512], BF16)
        stats = sbuf("stats", [128, 32], F32)
        sbf = sbuf("sbf", [64, 512], BF16)
        gbf = sbuf("gbf", [64, 512], BF16)
        hbf = sbuf("hbf", [128, 2 * 512], BF16)
        qsqbf = sbuf("qsqbf", [128, 512], BF16)
        smallf = sbuf("smallf", [128, 16], F32)
        w1col = sbuf("w1col", [128, 8], F32)
        b1col = sbuf("b1col", [128, 8], F32)
        w2col = sbuf("w2col", [128, 8], F32)
        binexp = sbuf("binexp", [128, 8], F32)
        svals = sbuf("svals", [1, 4], F32)
        constc = sbuf("constc", [128, 2], F32)
        eyef_sb = sbuf("eyef_sb", [128, 128], F32)

        ps_sc = ctx.enter_context(nc.psum_tensor("ps_sc", [128, 3072], F32))
        ps_pv = ctx.enter_context(nc.psum_tensor("ps_pv", [128, 1024], F32))

        plan = Plan(nc, ctx)
        E = plan.emit

        sc = [Buf() for _ in range(6)]
        pv = [Buf() for _ in range(2)]
        b_xt, b_x1t, b_tft = Buf(), Buf(), Buf()
        b_lnslot = [Buf(), Buf()]
        b_hbf = [Buf(), Buf()]
        b_sbf, b_gbf, b_qsq, b_stats = Buf(), Buf(), Buf(), Buf()

        init = contextlib.ExitStack()
        with init:
            rowbuf = sbuf("rowbuf", [1, 1024], F32, init)
            rowbuf2 = sbuf("rowbuf2", [1, 1024], F32, init)
            w1row = sbuf("w1row", [1, 1024], BF16, init)
            b2colb = sbuf("b2colb", [64, 1], BF16, init)
            binb = sbuf("binb", [64, 1], BF16, init)
            selpat_sb = sbuf("selpat_sb", [128, 8 * 64], BF16, init)
            w1row8 = sbuf("w1row8", [8, 128], F32, init)
            b1row8 = sbuf("b1row8", [8, 128], F32, init)
            w2row8 = sbuf("w2row8", [8, 128], F32, init)
            selpatT_sb = sbuf("selpatT_sb", [64, 8 * 128], BF16, init)
            ebf_sb = sbuf("ebf_sb", [64, 1024], BF16, init)

            tok_eye = E("sync", lambda e: e.dma_start(eye_sb[:, :], c_eye), inc="S")
            tok_eyef = E("sync", lambda e: e.dma_start(eyef_sb[:, :], c_eyef), inc="S")
            tok_eab = E("sync", lambda e: e.dma_start(eab_sb[:, :], c_eab), inc="S")
            t_sp = E("sync", lambda e: e.dma_start(selpat_sb[:, :], c_selpat), inc="S")
            t_spt = E("sync", lambda e: e.dma_start(selpatT_sb[:, :], c_selpatT), inc="S")
            t_ebf = E("sync", lambda e: e.dma_start(ebf_sb[:, :], c_ebf), inc="S")

            E("gpsimd", lambda e: e.memset(ones_b[:, :], 1.0), inc="K")
            t_ones = E("gpsimd", lambda e: e.memset(ones_f[:, :], 1.0), inc="K")
            E("vector", lambda e: e.memset(smallf[:, :], 1.0))
            E("vector", lambda e: e.memset(stats[:, :], 1.0))
            E("vector", lambda e: e.memset(constc[:, 0:1], CMARG))
            t_constc = E("vector", lambda e: e.memset(constc[:, 1:2], -5.0), inc="V")

            t_wq = E("gpsimd", lambda e: e.dma_start(wqkv_bf, wqkv), inc="G")
            t_wo = E("gpsimd", lambda e: e.dma_start(wout_bf, wout), inc="G")
            t_win = E("gpsimd", lambda e: e.dma_start(
                win[:, :].rearrange("k (c m) -> k c m", m=M),
                fwin.rearrange("(c k) m -> k c m", k=128)), inc="G")
            t_wfo = E("gpsimd", lambda e: e.dma_start(woutf[:, :], fwout), inc="G")
            t_w1r8 = E("gpsimd", lambda e: e.dma_start(
                w1row8[:, :], fw1.rearrange("(c p) -> c p", p=128)), inc="G")
            t_b1r8 = E("gpsimd", lambda e: e.dma_start(
                b1row8[:, :], fb1.rearrange("(c p) -> c p", p=128)), inc="G")
            t_w2r8 = E("gpsimd", lambda e: e.dma_start(
                w2row8[:, :], fw2.rearrange("(c p) -> c p", p=128)), inc="G")
            t_tw1 = E("tensor", lambda e: e.transpose(
                ps_sc[:, 0:8], w1row8[:, :], eyef_sb[0:8, 0:8]),
                waits=[t_w1r8, tok_eyef])
            t_tb1 = E("tensor", lambda e: e.transpose(
                ps_sc[:, 8:16], b1row8[:, :], eyef_sb[0:8, 0:8]), waits=[t_b1r8])
            t_tw2 = E("tensor", lambda e: e.transpose(
                ps_sc[:, 16:24], w2row8[:, :], eyef_sb[0:8, 0:8]), inc="P",
                waits=[t_w2r8])
            t_w1c = E("vector", lambda e: e.tensor_copy(w1col[:, :], ps_sc[:, 0:8]),
                      inc="V", waits=[t_tw2])
            t_b1c = E("vector", lambda e: e.tensor_copy(b1col[:, :], ps_sc[:, 8:16]),
                      inc="V")
            t_w2c = E("vector", lambda e: e.tensor_copy(w2col[:, :], ps_sc[:, 16:24]),
                      inc="V")
            sc[0].wrote(t_tw2)
            sc[0].read(t_w2c)
            t_w1r = E("gpsimd", lambda e: e.dma_start(
                w1row[:, :], fw1.rearrange("(one f) -> one f", one=1)), inc="G")
            t_b2b = E("gpsimd", lambda e: e.dma_start(
                b2colb[:, :], fb2.rearrange("(p one) -> p one", one=1)), inc="G")
            t_binb = E("gpsimd", lambda e: e.dma_start(
                binb[:, :], fbin.rearrange("(p one) -> p one", one=1)), inc="G")
            t_bout = E("gpsimd", lambda e: e.dma_start(
                rowbuf2[:, :], fbout.rearrange("(one f) -> one f", one=1)), inc="G")
            E("gpsimd", lambda e: e.dma_start(
                svals[:, 0:1], s_a.rearrange("(o f) -> o f", o=1)), inc="G")
            E("gpsimd", lambda e: e.dma_start(
                svals[:, 1:2], s_b.rearrange("(o f) -> o f", o=1)), inc="G")
            t_svals = E("gpsimd", lambda e: e.dma_start(
                svals[:, 2:3], s_dt.rearrange("(o f) -> o f", o=1)), inc="G")

            tok_wqkvT = None
            for c in range(8):
                tok_wqkvT = E("sync", lambda e, c=c: e.dma_start(
                    wqkvT[:, c * 3072:(c + 1) * 3072],
                    wqkv_bf[:, c * 128:(c + 1) * 128], transpose=True),
                    inc="S", waits=[t_wq])

            tok_bd = None
            for c in range(8):
                tok_bd = E("vector", lambda e, c=c: e.tensor_scalar(
                    bdw2[:, c * 64:(c + 1) * 64], selpat_sb[:, c * 64:(c + 1) * 64],
                    w2col[:, c:c + 1], None, op0=ALU.mult),
                    inc="V" if c == 7 else None, waits=[t_sp, t_w2c])

            E("tensor", lambda e: e.matmul(
                ps_sc[0:64, 0:512], ones_b[:, 0:64], w1row[:, 0:512],
                start=True, stop=True),
                waits=[t_w1r, t_ones] + sc[0].wgate())
            t_mmw1 = E("tensor", lambda e: e.matmul(
                ps_sc[0:64, 512:1024], ones_b[:, 0:64], w1row[:, 512:1024],
                start=True, stop=True), inc="P")
            sc[0].wrote(t_mmw1)
            sc[1].wrote(t_mmw1)
            t_ew1 = E("vector", lambda e: e.tensor_tensor(
                ew1[:, :], ps_sc[0:64, 0:1024], selpatT_sb[:, :], op=ALU.mult),
                inc="V", waits=[t_mmw1, t_spt])
            sc[0].read(t_ew1)
            sc[1].read(t_ew1)

            t_mmbe = None
            for c in range(8):
                t_mmbe = E("tensor", lambda e, c=c: e.matmul(
                    ps_pv[:, c:c + 1], ebf_sb[:, c * 128:(c + 1) * 128], binb[:, :],
                    start=True, stop=True),
                    inc="P" if c == 7 else None, waits=[t_ebf, t_binb])
            pv[0].wrote(t_mmbe)
            t_bex = E("vector", lambda e: e.tensor_copy(binexp[:, :], ps_pv[:, 0:8]),
                      inc="V", waits=[t_mmbe])
            pv[0].read(t_bex)
            t_b1a = E("vector", lambda e: e.tensor_tensor(
                b1p[:, :], w1col[:, :], binexp[:, :], op=ALU.mult), inc="V",
                waits=[t_w1c, t_bex])
            t_b1p = E("vector", lambda e: e.tensor_tensor(
                b1p[:, :], b1p[:, :], b1col[:, :], op=ALU.add), inc="V",
                waits=[t_b1c, t_b1a])

            E("tensor", lambda e: e.matmul(
                ps_pv[0:1, 512:768], b2colb[:, :], woutf[:, 0:256],
                start=True, stop=True), waits=[t_b2b, t_wfo])
            E("tensor", lambda e: e.matmul(
                ps_pv[0:1, 768:1024], b2colb[:, :], woutf[:, 256:512],
                start=True, stop=True))
            E("tensor", lambda e: e.matmul(
                ps_pv[0:1, 0:256], b2colb[:, :], woutf[:, 512:768],
                start=True, stop=True), waits=pv[0].wgate())
            t_mmb2 = E("tensor", lambda e: e.matmul(
                ps_pv[0:1, 256:512], b2colb[:, :], woutf[:, 768:1024],
                start=True, stop=True), inc="P")
            pv[0].wrote(t_mmb2)
            pv[1].wrote(t_mmb2)
            E("vector", lambda e: e.scalar_tensor_tensor(
                rowbuf[:, 0:512], ps_pv[0:1, 512:1024], 0.5, rowbuf2[:, 0:512],
                op0=ALU.mult, op1=ALU.bypass), waits=[t_mmb2, t_bout])
            t_hb0 = E("vector", lambda e: e.scalar_tensor_tensor(
                rowbuf[:, 512:1024], ps_pv[0:1, 0:512], 0.5, rowbuf2[:, 512:1024],
                op0=ALU.mult, op1=ALU.bypass), inc="V")
            pv[0].read(t_hb0)
            pv[1].read(t_hb0)
            t_hb = E("vector", lambda e: e.scalar_tensor_tensor(
                rowbuf[:, :], rowbuf2[:, :], 0.5, rowbuf[:, :],
                op0=ALU.mult, op1=ALU.add), inc="V")
            t_mmhb = None
            for half in range(2):
                t_mmhb = E("tensor", lambda e, half=half: e.matmul(
                    ps_sc[:, 1024 + half * 512:1536 + half * 512],
                    ones_f[0:1, :], rowbuf[:, half * 512:(half + 1) * 512],
                    start=True, stop=True),
                    inc="P" if half == 1 else None,
                    waits=[t_hb] + sc[2].wgate() + sc[3].wgate())
            sc[2].wrote(t_mmhb)
            sc[3].wrote(t_mmhb)
            t_hbbc = E("vector", lambda e: e.tensor_copy(
                hbbc[:, :], ps_sc[:, 1024:2048]), inc="V", waits=[t_mmhb])
            sc[2].read(t_hbbc)
            sc[3].read(t_hbbc)

            tok_init = [t_b1p, t_hbbc, tok_bd, t_win, t_ew1, tok_wqkvT,
                        t_mmbe, tok_eab, tok_eye, t_svals]
            if debug:
                E("sync", lambda e: e.dma_start(dbg2[:, 0:1024], hbbc[:, :]),
                  inc="S", waits=[t_hbbc])
                E("sync", lambda e: e.dma_start(dbg2[:, 1024:1032], b1p[:, :]),
                  inc="S", waits=[t_b1p])
                E("sync", lambda e: e.dma_start(dbg2[:, 1032:1040], binexp[:, :]),
                  inc="S", waits=[t_bex])
                E("sync", lambda e: e.dma_start(dbg2[:, 1040:1048], w2col[:, :]),
                  inc="S", waits=[t_w2c])
                E("sync", lambda e: e.dma_start(
                    dbg3[:, 0:512], bdw2[:, :].bitcast(mybir.dt.uint16)),
                  inc="S", waits=[tok_bd])
                E("sync", lambda e: e.dma_start(
                    dbg3[0:64, 512:1536], ew1[:, :].bitcast(mybir.dt.uint16)),
                  inc="S", waits=[t_ew1])
                E("sync", lambda e: e.dma_start(
                    dbg3[:, 1536:1600], win[:, 0:64].bitcast(mybir.dt.uint16)),
                  inc="S", waits=[t_win])

        def ln_stats_chain(tok_sums, tok_sqs):
            t1 = E("vector", lambda e: e.tensor_scalar(
                stats[:, 8:12], stats[:, 0:4], 1.0 / D, None, op0=ALU.mult),
                inc="V", waits=[tok_sums])
            t2 = E("vector", lambda e: e.tensor_scalar(
                stats[:, 12:16], stats[:, 4:8], 1.0 / D, EPS,
                op0=ALU.mult, op1=ALU.add), inc="V", waits=[tok_sqs])
            t3 = E("vector", lambda e: e.scalar_tensor_tensor(
                stats[:, 16:20], stats[:, 8:12], 1.0, stats[:, 8:12],
                op0=ALU.mult, op1=ALU.mult), inc="V", waits=[t1])
            t_v = E("vector", lambda e: e.tensor_sub(
                stats[:, 20:24], stats[:, 12:16], stats[:, 16:20]), inc="V",
                waits=[t2, t3])
            b_stats.read(t_v)
            t_ln = E("scalar", lambda e: e.activation(
                stats[:, 20:24], stats[:, 20:24], AF.Ln), inc="A", waits=[t_v])
            t_rs = E("scalar", lambda e: e.activation(
                stats[:, 24:28], stats[:, 20:24], AF.Exp, scale=-0.5), inc="A",
                waits=[t_ln])
            t_nm = E("vector", lambda e: e.scalar_tensor_tensor(
                stats[:, 28:32], stats[:, 8:12], -1.0, stats[:, 24:28],
                op0=ALU.mult, op1=ALU.mult), inc="V", waits=[t_rs, t1])
            return [t_rs, t_nm]

        def ln_apply(u, src_buf, tok_src, tok_stats):
            slot = u % 2
            tok = E("vector", lambda e, u=u, slot=slot: e.tensor_scalar(
                lntok[:, slot * 1024:(slot + 1) * 1024],
                src_buf[:, u * 1024:(u + 1) * 1024],
                stats[:, 24 + u:25 + u], stats[:, 28 + u:29 + u],
                op0=ALU.mult, op1=ALU.add),
                inc="V",
                waits=[tok_src, tok_stats, b_lnslot[slot].wgate()])
            b_lnslot[slot].wrote(tok)
            return tok

        def transpose_subtile(u, dst, dst_buf, tok_slot, srcbuf):
            slot = u % 2
            toks = []
            for half in range(2):
                bank = 4 + half
                t_tp = None
                for cc in range(4):
                    c = half * 4 + cc
                    t_tp = E("tensor", lambda e, c=c, cc=cc, slot=slot, half=half, bank=bank: e.transpose(
                        ps_sc[:, bank * 512 + cc * 128: bank * 512 + (cc + 1) * 128],
                        srcbuf[:, slot * 1024 + c * 128: slot * 1024 + (c + 1) * 128],
                        eyef_sb[:, :]),
                        inc="P" if cc == 3 else None,
                        waits=[tok_slot, sc[bank].wgate()] if cc == 0 else [])
                sc[bank].wrote(t_tp)
                t_ev = E("vector", lambda e, u=u, half=half, bank=bank: e.tensor_copy(
                    dst[:, :].rearrange("p (c w) -> p c w", w=512)
                       [:, half * 4:(half + 1) * 4, u * 128:(u + 1) * 128],
                    ps_sc[:, bank * 512:bank * 512 + 512]
                        .rearrange("p (c w) -> p c w", w=128)),
                    inc="V", waits=[t_tp, dst_buf.wgate() if u == 0 and half == 0 else None])
                sc[bank].read(t_ev)
                toks.append(t_ev)
            return toks

        def ffn_block(tok_tft, dst_buf, dst_tok_buf, src_buf, src_tok_buf, tok_src, sum_cols):
            t_mm = None
            for c in range(8):
                t_mm = E("tensor", lambda e, c=c: e.matmul(
                    ps_sc[0:64, 0:512], win[:, c * 64:(c + 1) * 64],
                    tft[:, c * 512:(c + 1) * 512],
                    start=(c == 0), stop=(c == 7), skip_group_check=True),
                    inc="P" if c == 7 else None,
                    waits=[tok_tft, sc[0].wgate(), tok_init] if c == 0 else [])
            sc[0].wrote(t_mm)
            b_tft.read(t_mm)
            t_sb = E("vector", lambda e: e.tensor_copy(sbf[:, :], ps_sc[0:64, 0:512]),
                     inc="V", waits=[t_mm, b_sbf.wgate()])
            sc[0].read(t_sb)
            b_sbf.wrote(t_sb)
            t_g = None
            for c in range(8):
                bank = 1 + (c % 3)
                t_h = E("tensor", lambda e, c=c, bank=bank: e.matmul(
                    ps_sc[:, bank * 512:(bank + 1) * 512],
                    ew1[:, c * 128:(c + 1) * 128], sbf[:, :],
                    start=True, stop=True, skip_group_check=True),
                    inc="P", waits=[t_sb, sc[bank].wgate()])
                sc[bank].wrote(t_h)
                b_sbf.read(t_h)
                t_r = E("vector", lambda e, c=c, bank=bank: e.tensor_scalar(
                    hbf[:, (c % 2) * 512:(c % 2 + 1) * 512],
                    ps_sc[:, bank * 512:(bank + 1) * 512],
                    b1p[:, c:c + 1], 0.0, op0=ALU.add, op1=ALU.max),
                    inc="V", waits=[t_h, b_hbf[c % 2].wgate()])
                sc[bank].read(t_r)
                b_hbf[c % 2].wrote(t_r)
                t_g = E("tensor", lambda e, c=c: e.matmul(
                    ps_sc[0:64, 0:512], bdw2[:, c * 64:(c + 1) * 64],
                    hbf[:, (c % 2) * 512:(c % 2 + 1) * 512],
                    start=(c == 0), stop=(c == 7), skip_group_check=True),
                    inc="P" if c == 7 else None,
                    waits=[t_r, sc[0].wgate() if c == 0 else None])
                b_hbf[c % 2].read(t_g)
            sc[0].wrote(t_g)
            t_gb = E("vector", lambda e: e.tensor_copy(gbf[:, :], ps_sc[0:64, 0:512]),
                     inc="V", waits=[t_g, b_gbf.wgate()])
            sc[0].read(t_gb)
            b_gbf.wrote(t_gb)
            sum_toks = [None] * 4
            for u in range(4):
                t_q = None
                for half in range(2):
                    bank = 2 + half
                    t_mo = E("tensor", lambda e, u=u, half=half, bank=bank: e.matmul(
                        ps_sc[:, bank * 512:(bank + 1) * 512],
                        gbf[:, u * 128:(u + 1) * 128],
                        woutf[:, half * 512:(half + 1) * 512],
                        start=True, stop=True, skip_group_check=True),
                        inc="P", waits=[t_gb, sc[bank].wgate()])
                    sc[bank].wrote(t_mo)
                    b_gbf.read(t_mo)
                    t_q = E("vector", lambda e, u=u, half=half, bank=bank: e.scalar_tensor_tensor(
                        dst_buf[:, u * 1024 + half * 512: u * 1024 + (half + 1) * 512],
                        ps_sc[:, bank * 512:(bank + 1) * 512], 0.5,
                        src_buf[:, u * 1024 + half * 512: u * 1024 + (half + 1) * 512],
                        op0=ALU.mult, op1=ALU.add),
                        inc="V",
                        waits=[t_mo, tok_src, tok_init if u == 0 and half == 0 else None,
                               dst_tok_buf.wgate() if u == 0 and half == 0 else None])
                    sc[bank].read(t_q)
                    src_tok_buf.read(t_q)
                col = sum_cols[u]
                t_s = E("vector", lambda e, u=u, col=col: e.scalar_tensor_tensor(
                    dst_buf[:, u * 1024:(u + 1) * 1024],
                    dst_buf[:, u * 1024:(u + 1) * 1024], 1.0, hbbc[:, :],
                    op0=ALU.mult, op1=ALU.add,
                    accum_out=stats[:, col:col + 1]),
                    inc="V", waits=[t_q, b_stats.wgate() if u == 0 else None])
                sum_toks[u] = t_s
            dst_tok_buf.wrote(sum_toks[3])
            b_stats.wrote(sum_toks[3])
            return sum_toks[3], sum_toks

        ab = contextlib.ExitStack()
        with ab:
            kT = sbuf("kT", [128, 4 * T], BF16, ab)
            vv = sbuf("vv", [128, GH * 32 * 65], BF16, ab)
            qt = sbuf("qt", [128, 4 * 512], BF16, ab)
            pt = sbuf("pt", [128, 6144], BF16, ab)
            y1t = sbuf("y1t", [128, 8 * 512], BF16, ab)
            masks_sb = sbuf("masks_sb", [128, 4 * QT], BF16, ab)
            rawstage = sbuf("rawstage", [128, 4 * 512], BF16, ab)
            lrow = sbuf("lrow", [1, 512], F32, ab)
            cb = sbuf("cb", [128, 8], F32, ab)
            qmaxsc = sbuf("qmaxsc", [128, 4], F32, ab)
            kmaxsc = sbuf("kmaxsc", [128, 4], F32, ab)
            mk2 = sbuf("mk2", [128, 4], F32, ab)
            negc = sbuf("negc", [128, 4], F32, ab)

            b_qt, b_pt, b_y1t, b_raw, b_cb = Buf(), Buf(), Buf(), Buf(), Buf()
            b_lrow = [Buf(), Buf()]
            b_kv = Buf()
            b_cmax = Buf()
            lri = [0]

            tok_masks = E("sync", lambda e: e.dma_start(masks_sb[:, :], c_masks),
                          inc="S", waits=[tok_init])
            E("vector", lambda e: e.memset(qmaxsc[:, :], 1.0), waits=[tok_init])
            E("vector", lambda e: e.memset(kmaxsc[:, :], 1.0))
            E("vector", lambda e: e.memset(negc[:, :], 1.0))
            E("vector", lambda e: e.memset(cb[:, :], 0.0))

            def group_pass(g, tok_start):
                t_mk = E("vector", lambda e: e.memset(mk2[:, :], 0.0), inc="V",
                         waits=[tok_start])
                t_vv = E("gpsimd", lambda e: e.memset(vv[:, :], 1.0), inc="K",
                         waits=[tok_start, b_kv.wgate()])
                b_kv.wrote(t_vv)
                b_cmax.wrote(t_mk)
                tok_tile_end = None
                tok_pv = [[None, None], [None, None]]
                tok_maskw = [[None, None], [None, None]]

                def load_x(ti, tok_first):
                    tok = E("sync", lambda e, ti=ti: e.dma_start(
                        xt[:, :].rearrange("p (u d) -> p u d", u=4),
                        x_in.rearrange("(tt u p) d -> tt p u d", u=4, p=128)[ti]),
                        inc="S", waits=[b_xt.wgate(), tok_first])
                    b_xt.wrote(tok)
                    return tok

                def ln1_stats(ti, tok_x):
                    su_toks, sq_toks = [], []
                    for u in range(4):
                        t_su = E("vector", lambda e, u=u: e.reduce_sum(
                            stats[:, u:u + 1], xt[:, u * 1024:(u + 1) * 1024],
                            axis=AX.X), inc="V",
                            waits=[tok_x, b_stats.wgate() if u == 0 else None])
                        t_sq = E("scalar", lambda e, u=u: e.activation(
                            x1t[:, u * 1024:(u + 1) * 1024],
                            xt[:, u * 1024:(u + 1) * 1024],
                            AF.Square, accum_out=stats[:, 4 + u:5 + u]),
                            inc="A", waits=[tok_x, b_x1t.wgate() if u == 0 else None,
                                            b_stats.wgate() if u == 0 else None])
                        b_x1t.read(t_sq)
                        b_xt.read(t_su)
                        b_xt.read(t_sq)
                        su_toks.append(t_su)
                        sq_toks.append(t_sq)
                    b_stats.wrote(sq_toks[3])
                    return ln_stats_chain(su_toks[3], sq_toks[3])

                def emit_attention(a_ti, a_kv, a_qt, a_cb, mid_emit=None):
                    raw_toks = []
                    nCH = 4 * (a_ti + 1)
                    wins = [list(range(s, min(s + 3, nCH))) for s in range(0, nCH, 3)]
                    nW = len(wins)
                    for ph in range(4):
                        pp = ph
                        tok_expw = [[None, None], [None, None]]
                        tok_pv_stop = [None, None]

                        def emit_qk(w, hh):
                            t_qk = None
                            for ji, j in enumerate(wins[w]):
                                bank = 3 * hh + j % 3
                                t_qk = E("tensor", lambda e, j=j, pp=pp, hh=hh, bank=bank: e.matmul(
                                    ps_sc[:, bank * 512:(bank + 1) * 512],
                                    kT[hh * 64:hh * 64 + 64,
                                       pp * T + j * 128: pp * T + (j + 1) * 128],
                                    qt[hh * 64:hh * 64 + 64,
                                       pp * 512:(pp + 1) * 512],
                                    start=True, stop=True, skip_group_check=True),
                                    inc="P" if ji == len(wins[w]) - 1 else None,
                                    waits=[sc[3 * hh + (j + ji0) % 3].wgate()
                                           for ji0 in range(len(wins[w]))] +
                                          [a_kv, a_qt] if ji == 0 else [])
                                if ji == len(wins[w]) - 1:
                                    for jj in wins[w]:
                                        sc[3 * hh + jj % 3].wrote(t_qk)
                                    b_qt.read(t_qk)
                                    b_kv.read(t_qk)
                            return t_qk

                        def emit_exp(w, hh, t_qk):
                            lh = 2 * pp + hh
                            slot = w % 2
                            j0 = wins[w][0]
                            nwc = len(wins[w])
                            b0 = 3 * hh + j0 % 3
                            t_ex = E("scalar", lambda e, hh=hh, lh=lh, slot=slot, b0=b0, nwc=nwc: e.activation(
                                pt[:, hh * 3072 + slot * 1536: hh * 3072 + slot * 1536 + nwc * 512],
                                ps_sc[:, b0 * 512:(b0 + nwc) * 512], AF.Exp,
                                bias=cb[:, lh:lh + 1]),
                                inc="A",
                                waits=[t_qk, a_cb, tok_pv[hh][slot],
                                       tok_maskw[hh][slot]])
                            for jj in wins[w]:
                                sc[3 * hh + jj % 3].read(t_ex)
                            b_cb.read(t_ex)
                            tok_expw[hh][slot] = t_ex
                            if wins[w][-1] >= nCH - 4:
                                t_m = None
                                first = True
                                for ji, j in enumerate(wins[w]):
                                    k = j - (nCH - 4)
                                    if k < 0:
                                        continue
                                    t_m = E("gpsimd", lambda e, hh=hh, slot=slot, ji=ji, k=k: e.tensor_tensor(
                                        pt[:, hh * 3072 + slot * 1536 + ji * 512:
                                           hh * 3072 + slot * 1536 + (ji + 1) * 512],
                                        pt[:, hh * 3072 + slot * 1536 + ji * 512:
                                           hh * 3072 + slot * 1536 + (ji + 1) * 512],
                                        masks_sb[:, k * 512:(k + 1) * 512],
                                        op=ALU.mult),
                                        inc="K",
                                        waits=[t_ex, tok_masks] if first else [])
                                    first = False
                                if t_m is not None:
                                    tok_maskw[hh][slot] = t_m
                                    tok_expw[hh][slot] = t_m
                            return tok_expw[hh][slot]

                        def emit_pv(w, hh):
                            lh = 2 * pp + hh
                            slot = w % 2
                            t_pv = None
                            for ji, j in enumerate(wins[w]):
                                is_start = (w == 0 and ji == 0)
                                is_stop = (w == nW - 1 and ji == len(wins[w]) - 1)
                                t_pv = E("tensor", lambda e, ji=ji, j=j, lh=lh, hh=hh, slot=slot,
                                         is_start=is_start, is_stop=is_stop: e.matmul(
                                    ps_pv[0:65, hh * 512:(hh + 1) * 512],
                                    vv[:, lh * 2080 + j * 65: lh * 2080 + (j + 1) * 65],
                                    pt[:, hh * 3072 + slot * 1536 + ji * 512:
                                       hh * 3072 + slot * 1536 + (ji + 1) * 512],
                                    start=is_start, stop=is_stop,
                                    skip_group_check=True),
                                    inc="P" if ji == len(wins[w]) - 1 else None,
                                    waits=[tok_expw[hh][slot],
                                           pv[hh].wgate() if is_start else None]
                                    if ji == 0 else [])
                                if is_stop:
                                    tok_pv_stop[hh] = t_pv
                            tok_pv[hh][slot] = t_pv
                            b_kv.read(t_pv)
                            return t_pv

                        for w in range(nW):
                            for hh in (0, 1):
                                t_qk = emit_qk(w, hh)
                                emit_exp(w, hh, t_qk)
                            if w > 0:
                                for hh in (0, 1):
                                    emit_pv(w - 1, hh)
                        for hh in (0, 1):
                            emit_pv(nW - 1, hh)
                        for hh in (0, 1):
                            lh = 2 * pp + hh
                            h = g * GH + lh
                            pv[hh].wrote(tok_pv_stop[hh])
                            t_re = E("vector", lambda e, hh=hh, pp=pp: e.tensor_copy(
                                rawstage[hh * 64:(hh + 1) * 64, pp * 512:(pp + 1) * 512],
                                ps_pv[0:64, hh * 512:(hh + 1) * 512]),
                                inc="V",
                                waits=[tok_pv_stop[hh],
                                       b_raw.wgate() if (pp == 0 and hh == 0) else None])
                            t_le = E("vector", lambda e, hh=hh: e.tensor_copy(
                                lrow[:, 0:512],
                                ps_pv[64:65, hh * 512:(hh + 1) * 512]),
                                inc="V", waits=[b_lrow[0].wgate()])
                            b_lrow[0].wrote(t_le)
                            t_ld = E("sync", lambda e, h=h, ti=a_ti: e.dma_start(
                                Ls[h:h + 1, ti * 512:(ti + 1) * 512],
                                lrow[:, 0:512]),
                                inc="S", waits=[t_le])
                            b_lrow[0].read(t_ld)
                            pv[hh].read(t_re)
                            pv[hh].read(t_le)
                            raw_toks.append(t_re)
                        if ph == 0 and mid_emit is not None:
                            mid_emit()
                    t_rd = E("sync", lambda e, ti=a_ti, g=g: e.dma_start(
                        rawTs.rearrange("(gg pp half hd) tt -> gg (half hd) pp tt",
                                        gg=2, pp=4, half=2)[g][:, :, ti * 512:(ti + 1) * 512],
                        rawstage[:, :].rearrange("p (pp w) -> p pp w", w=512)),
                        inc="S", waits=[raw_toks])
                    b_raw.wrote(t_rd)
                    return t_rd

                def emit_t1(ti, tok_x, tok_st):
                    sp_toks = []
                    for u in range(4):
                        t_ap = ln_apply(u, xt, tok_x, tok_st)
                        b_xt.read(t_ap)
                        t_sp1 = E("sync", lambda e, ti=ti, u=u: e.dma_start(
                            t1s[ti * QT + u * 128: ti * QT + (u + 1) * 128, :],
                            lntok[:, (u % 2) * 1024:(u % 2 + 1) * 1024]),
                            inc="S", waits=[t_ap])
                        b_lnslot[u % 2].read(t_sp1)
                        sp_toks.append(t_sp1)
                    t_tl = None
                    for c in range(8):
                        t_tl = E("sync", lambda e, ti=ti, c=c: e.dma_start(
                            tft[:, c * 512:(c + 1) * 512],
                            t1s[ti * QT:(ti + 1) * QT, c * 128:(c + 1) * 128],
                            transpose=True),
                            inc="S",
                            waits=[sp_toks, b_tft.wgate()] if c == 0 else [])
                    b_tft.wrote(t_tl)
                    return t_tl

                pend_x = None
                pend_st = None
                pend_tft = None
                pend_att = None
                tok_tile_end = None
                for ti in range(NT):
                    if g == 0:
                        if ti == 0:
                            tok_x = load_x(0, tok_start)
                            tok_st = ln1_stats(0, tok_x)
                            tok_tft = emit_t1(0, tok_x, tok_st)
                        else:
                            tok_x, tok_tft = pend_x, pend_tft
                        tok_y1t = None
                    else:
                        if ti == 0:
                            t_y = None
                            for c in range(8):
                                t_y = E("sync", lambda e, ti=ti, c=c: e.dma_start(
                                    y1t[:, c * 512:(c + 1) * 512],
                                    y1s[ti * QT:(ti + 1) * QT, c * 128:(c + 1) * 128],
                                    transpose=True),
                                    inc="S",
                                    waits=[b_y1t.wgate(), tok_start] if c == 0 else [])
                            tok_y1t = t_y
                            b_y1t.wrote(t_y)
                        else:
                            tok_y1t = pend_x
                            b_y1t.wrote(tok_y1t)

                    if g == 0:
                        t_x1, s2 = ffn_block([tok_tft], x1t, b_x1t,
                                             xt, b_xt, tok_x, [0, 1, 2, 3])
                        t_x1d = E("sync", lambda e, ti=ti: e.dma_start(
                            x1s.rearrange("(tt u p) d -> tt p u d", u=4, p=128)[ti],
                            x1t[:, :].rearrange("p (u d) -> p u d", u=4)),
                            inc="S", waits=[t_x1])
                        b_x1t.read(t_x1d)
                        sq2 = []
                        for u in range(4):
                            t_sq = E("scalar", lambda e, u=u: e.activation(
                                xt[:, u * 1024:(u + 1) * 1024],
                                x1t[:, u * 1024:(u + 1) * 1024],
                                AF.Square, accum_out=stats[:, 4 + u:5 + u]),
                                inc="A", waits=[s2[u],
                                                b_stats.wgate() if u == 0 else None])
                            b_x1t.read(t_sq)
                            b_xt.read(t_sq)
                            sq2.append(t_sq)
                        b_stats.wrote(sq2[3])
                        tok_st2 = ln_stats_chain(s2[3], sq2[3])

                        hold = {}

                        def mid(ti=ti, t_x1=t_x1, tok_st2=tok_st2, hold=hold):
                            sp_toks = []
                            for u in range(4):
                                t_ap = ln_apply(u, x1t, t_x1, tok_st2)
                                b_x1t.read(t_ap)
                                t_spill = E("sync", lambda e, ti=ti, u=u: e.dma_start(
                                    y1s[ti * QT + u * 128: ti * QT + (u + 1) * 128, :],
                                    lntok[:, (u % 2) * 1024:(u % 2 + 1) * 1024]),
                                    inc="S", waits=[t_ap])
                                b_lnslot[u % 2].read(t_spill)
                                sp_toks.append(t_spill)
                            t_yl = None
                            for c in range(8):
                                t_yl = E("sync", lambda e, ti=ti, c=c: e.dma_start(
                                    y1t[:, c * 512:(c + 1) * 512],
                                    y1s[ti * QT:(ti + 1) * QT, c * 128:(c + 1) * 128],
                                    transpose=True),
                                    inc="S",
                                    waits=[sp_toks, b_y1t.wgate()] if c == 0 else [])
                            hold["y1t"] = t_yl

                        if pend_att is not None:
                            tok_tile_end = emit_attention(*pend_att, mid_emit=mid)
                        else:
                            mid()
                        tok_y1t = hold["y1t"]
                        b_y1t.wrote(tok_y1t)
                    else:
                        if pend_att is not None:
                            tok_tile_end = emit_attention(*pend_att)

                    qt_toks = []
                    kv_toks = []
                    for pp in range(4):
                        hA = g * GH + 2 * pp
                        qb = pp % 2
                        t_mq = None
                        for c in range(8):
                            t_mq = E("tensor", lambda e, c=c, hA=hA, qb=qb: e.matmul(
                                ps_sc[:, qb * 512:(qb + 1) * 512],
                                wqkvT[:, c * 3072 + hA * 64: c * 3072 + hA * 64 + 128],
                                y1t[:, c * 512:(c + 1) * 512],
                                start=(c == 0), stop=(c == 7), skip_group_check=True),
                                inc="P" if c == 7 else None,
                                waits=[tok_y1t, sc[qb].wgate(),
                                       tok_init] if c == 0 else [])
                        sc[qb].wrote(t_mq)
                        b_y1t.read(t_mq)
                        t_qe = E("vector", lambda e, pp=pp, qb=qb: e.tensor_scalar(
                            qt[:, pp * 512:(pp + 1) * 512],
                            ps_sc[:, qb * 512:(qb + 1) * 512], 0.125, None,
                            op0=ALU.mult), inc="V",
                            waits=[t_mq, b_qt.wgate() if pp == 0 else None])
                        t_qs = E("vector", lambda e, pp=pp: e.tensor_tensor(
                            qsqbf[:, :], qt[:, pp * 512:(pp + 1) * 512],
                            qt[:, pp * 512:(pp + 1) * 512], op=ALU.mult),
                            inc="V", waits=[t_qe, b_qsq.wgate()])
                        sc[qb].read(t_qe)
                        sc[qb].read(t_qs)
                        b_qsq.wrote(t_qs)
                        qt_toks.append(t_qe)
                        t_eq = E("tensor", lambda e: e.matmul(
                            ps_pv[:, 0:512], eab_sb[:, :], qsqbf[:, :],
                            start=True, stop=True, skip_group_check=True),
                            inc="P", waits=[t_qs, pv[0].wgate()])
                        pv[0].wrote(t_eq)
                        b_qsq.read(t_eq)
                        E("vector", lambda e, pp=pp: e.reduce_max(
                            qmaxsc[0:1, pp:pp + 1], ps_pv[0:1, 0:512], axis=AX.X),
                            waits=[t_eq, b_cmax.wgate() if pp == 0 else None])
                        t_r2 = E("vector", lambda e, pp=pp: e.reduce_max(
                            qmaxsc[64:65, pp:pp + 1], ps_pv[64:65, 0:512], axis=AX.X),
                            inc="V")
                        pv[0].read(t_r2)
                        kb = 2 + pp % 2
                        t_mk2 = None
                        for c in range(8):
                            t_mk2 = E("tensor", lambda e, c=c, hA=hA, kb=kb: e.matmul(
                                ps_sc[:, kb * 512:(kb + 1) * 512],
                                wqkvT[:, c * 3072 + 1024 + hA * 64:
                                      c * 3072 + 1024 + hA * 64 + 128],
                                y1t[:, c * 512:(c + 1) * 512],
                                start=(c == 0), stop=(c == 7), skip_group_check=True),
                                inc="P" if c == 7 else None,
                                waits=[tok_y1t, sc[kb].wgate(),
                                       b_kv.w if (ti == 0 and pp == 0) else None]
                                if c == 0 else [])
                        sc[kb].wrote(t_mk2)
                        b_y1t.read(t_mk2)
                        t_ke = E("vector", lambda e, pp=pp, ti=ti, kb=kb: e.tensor_copy(
                            kT[:, pp * T + ti * 512: pp * T + (ti + 1) * 512],
                            ps_sc[:, kb * 512:(kb + 1) * 512]),
                            inc="V", waits=[t_mk2])
                        t_ks = E("vector", lambda e, pp=pp, ti=ti: e.tensor_tensor(
                            qsqbf[:, :], kT[:, pp * T + ti * 512: pp * T + (ti + 1) * 512],
                            kT[:, pp * T + ti * 512: pp * T + (ti + 1) * 512],
                            op=ALU.mult),
                            inc="V", waits=[t_ke, b_qsq.wgate()])
                        sc[kb].read(t_ke)
                        sc[kb].read(t_ks)
                        b_qsq.wrote(t_ks)
                        kv_toks.append(t_ke)
                        t_ek = E("tensor", lambda e: e.matmul(
                            ps_pv[:, 512:1024], eab_sb[:, :], qsqbf[:, :],
                            start=True, stop=True, skip_group_check=True),
                            inc="P", waits=[t_ks, pv[1].wgate()])
                        pv[1].wrote(t_ek)
                        b_qsq.read(t_ek)
                        E("vector", lambda e, pp=pp: e.reduce_max(
                            kmaxsc[0:1, pp:pp + 1], ps_pv[0:1, 512:1024], axis=AX.X),
                            waits=[t_ek])
                        t_r4 = E("vector", lambda e, pp=pp: e.reduce_max(
                            kmaxsc[64:65, pp:pp + 1], ps_pv[64:65, 512:1024],
                            axis=AX.X), inc="V")
                        pv[1].read(t_r4)
                        b_cmax.read(t_r4)
                    b_qt.wrote(qt_toks[-1])
                    for u in range(4):
                        j = ti * 4 + u
                        vb = u % 4
                        t_mv = None
                        for c in range(8):
                            t_mv = E("tensor", lambda e, c=c, u=u, g=g, vb=vb: e.matmul(
                                ps_sc[:, vb * 512:(vb + 1) * 512],
                                y1t[:, c * 512 + u * 128: c * 512 + (u + 1) * 128],
                                wqkvT[:, c * 3072 + 2048 + g * 512:
                                      c * 3072 + 2048 + (g + 1) * 512],
                                start=(c == 0), stop=(c == 7), skip_group_check=True),
                                inc="P" if c == 7 else None,
                                waits=[tok_y1t, sc[vb].wgate()] if c == 0 else [])
                        sc[vb].wrote(t_mv)
                        b_y1t.read(t_mv)
                        t_ve = E("vector", lambda e, j=j, vb=vb: e.tensor_copy(
                            vv[:, :].rearrange("p (lh w) -> p lh w", w=32 * 65)
                              [:, :, j * 65:j * 65 + 64],
                            ps_sc[:, vb * 512:(vb + 1) * 512]
                                .rearrange("p (lh w) -> p lh w", w=64)),
                            inc="V",
                            waits=[t_mv, b_kv.w if (ti == 0 and u == 0) else None])
                        sc[vb].read(t_ve)
                        kv_toks.append(t_ve)

                    t_mku = E("vector", lambda e: e.tensor_max(
                        mk2[:, :], mk2[:, :], kmaxsc[:, :]), inc="V",
                        waits=[t_r4, t_mk])
                    t_nc0 = E("vector", lambda e: e.tensor_tensor(
                        negc[:, :], qmaxsc[:, :], mk2[:, :], op=ALU.mult),
                        inc="V", waits=[t_mku, t_r2])
                    t_nc1 = E("scalar", lambda e: e.activation(
                        negc[:, :], negc[:, :], AF.Ln), inc="A", waits=[t_nc0])
                    t_nc2 = E("scalar", lambda e: e.activation(
                        negc[:, :], negc[:, :], AF.Exp, scale=0.5,
                        bias=constc[:, 0:1]), inc="A", waits=[t_constc, t_nc1])
                    t_nc3 = E("vector", lambda e: e.tensor_scalar_mul(
                        negc[:, :], negc[:, :], -1.0), inc="V", waits=[t_nc2])
                    b_cmax.wrote(t_nc3)
                    t_cbm = None
                    for lh in range(GH):
                        pp, half = lh // 2, lh % 2
                        t_cbm = E("tensor", lambda e, lh=lh, pp=pp, half=half: e.matmul(
                            ps_pv[:, lh:lh + 1],
                            ones_f[half * 64:half * 64 + 1, :],
                            negc[half * 64:half * 64 + 1, pp:pp + 1],
                            start=True, stop=True, skip_group_check=True),
                            inc="P" if lh == GH - 1 else None,
                            waits=[t_nc3, pv[0].wgate()] if lh == 0 else [])
                    pv[0].wrote(t_cbm)
                    b_cmax.read(t_cbm)
                    t_cb = E("vector", lambda e: e.tensor_copy(
                        cb[:, 0:8], ps_pv[:, 0:8]), inc="V",
                        waits=[t_cbm, b_cb.wgate()])
                    pv[0].read(t_cb)
                    b_cb.wrote(t_cb)

                    if ti + 1 < NT:
                        if g == 0:
                            pend_x = load_x(ti + 1, None)
                            pend_st = ln1_stats(ti + 1, pend_x)
                            pend_tft = emit_t1(ti + 1, pend_x, pend_st)
                        else:
                            t_y = None
                            for c in range(8):
                                t_y = E("sync", lambda e, ti=ti, c=c: e.dma_start(
                                    y1t[:, c * 512:(c + 1) * 512],
                                    y1s[(ti + 1) * QT:(ti + 2) * QT,
                                        c * 128:(c + 1) * 128],
                                    transpose=True),
                                    inc="S",
                                    waits=[b_y1t.wgate()] if c == 0 else [])
                            pend_x = t_y
                    pend_att = (ti, kv_toks, qt_toks, t_cb)
                tok_tile_end = emit_attention(*pend_att)
                return tok_tile_end

            tok_A = group_pass(0, [tok_init, tok_masks])
            tok_Bp = group_pass(1, [tok_A])
            tok_AB = [tok_Bp, ("P", plan.cnt["P"]), ("A", plan.cnt["A"]),
                      ("V", plan.cnt["V"]), ("S", plan.cnt["S"])]

        cstk = contextlib.ExitStack()
        with cstk:
            woutT = sbuf("woutT", [128, 8 * 1024], BF16, cstk)
            rawin = sbuf("rawin", [128, 8 * 512], BF16, cstk)
            scaledT = sbuf("scaledT", [128, 8 * 512], BF16, cstk)
            epat_sb = sbuf("epat_sb", [16, 8 * 128], F32, cstk)
            hpat_sb = sbuf("hpat_sb", [16, 128], F32, cstk)
            Lm = sbuf("Lm", [128, 512], F32, cstk)
            rcl = sbuf("rcl", [128, 512], F32, cstk)
            S128 = sbuf("S128", [128, 512], F32, cstk)
            Smat = sbuf("Smat", [16, T], F32, cstk)
            gA = sbuf("gA", [128, 512], F32, cstk)
            gB = sbuf("gB", [128, 512], F32, cstk)
            gC = sbuf("gC", [128, 512], F32, cstk)
            gD = sbuf("gD", [128, 512], F32, cstk)
            srow = sbuf("srow", [1, 128], F32, cstk)
            rs16 = sbuf("rs16", [16, 1], F32, cstk)
            xt2 = sbuf("xt2", [128, 4 * 1024], F32, cstk)
            lnf32 = sbuf("lnf32", [128, 2 * 1024], F32, cstk)

            b_rawin, b_scaled = Buf(), Buf()
            b_lnf = [Buf(), Buf()]

            t_ep = E("sync", lambda e: e.dma_start(epat_sb[:, :], c_epat),
                     inc="S", waits=[tok_AB])
            tok_woutT = None
            for c in range(8):
                tok_woutT = E("sync", lambda e, c=c: e.dma_start(
                    woutT[:, c * 1024:(c + 1) * 1024],
                    wout_bf[:, c * 128:(c + 1) * 128], transpose=True),
                    inc="S", waits=[t_wo, tok_AB] if c == 0 else [])
            t_lm = E("sync", lambda e: e.dma_start(
                Lm[:, :], Ls.rearrange("h (ee w) -> (h ee) w", ee=8)),
                inc="S", waits=[tok_AB])
            t_hp = E("sync", lambda e: e.dma_start(hpat_sb[:, :], c_hpat),
                     inc="S", waits=[tok_AB])

            t_sv = None
            for i in range(3):
                t_sv = E("tensor", lambda e, i=i: e.matmul(
                    ps_pv[:, 8 + i:9 + i], ones_f[0:1, :], svals[:, i:i + 1],
                    start=True, stop=True, skip_group_check=True),
                    inc="P" if i == 2 else None,
                    waits=[t_svals, pv[0].wgate(), tok_AB] if i == 0 else [])
            pv[0].wrote(t_sv)
            t_svc = E("vector", lambda e: e.tensor_copy(
                smallf[:, 0:3], ps_pv[:, 8:11]), inc="V", waits=[t_sv])
            pv[0].read(t_svc)
            acol, bcol, dtcol = smallf[:, 0:1], smallf[:, 1:2], smallf[:, 2:3]
            k1 = E("vector", lambda e: e.tensor_scalar(
                smallf[:, 3:4], dtcol, 1.0 / 12.5, None, op0=ALU.mult),
                inc="V", waits=[t_svc])
            k2 = E("vector", lambda e: e.tensor_tensor(
                smallf[:, 4:5], smallf[:, 3:4], bcol, op=ALU.mult),
                inc="V", waits=[k1])
            k3 = E("vector", lambda e: e.tensor_scalar(
                smallf[:, 4:5], smallf[:, 4:5], 1.0, None, op0=ALU.add),
                inc="V", waits=[k2])
            k4 = E("vector", lambda e: e.reciprocal(smallf[:, 5:6], smallf[:, 4:5]),
                inc="V", waits=[k3])
            k5 = E("vector", lambda e: e.tensor_tensor(
                smallf[:, 6:7], smallf[:, 3:4], smallf[:, 5:6], op=ALU.mult),
                inc="V", waits=[k4])
            k6 = E("vector", lambda e: e.tensor_tensor(
                smallf[:, 9:10], acol, smallf[:, 6:7], op=ALU.mult),
                inc="V", waits=[k5])
            t_kd = E("vector", lambda e: e.tensor_scalar(
                smallf[:, 10:11], smallf[:, 6:7], -1.0, None, op0=ALU.mult),
                inc="V", waits=[k6])
            akd = smallf[:, 9:10]
            negkd = smallf[:, 10:11]
            dtc = smallf[:, 2:3]

            t_rl = E("vector", lambda e: e.reciprocal(rcl[:, :], Lm[:, :]),
                     inc="V", waits=[t_lm])
            tEa = E("vector", lambda e: e.tensor_tensor(
                gA[:, :], Lm[:, :], rcl[:, :], op=ALU.mult), inc="V",
                waits=[t_rl])
            m1 = E("vector", lambda e: e.reduce_max(
                smallf[:, 11:12], gA[:, :], axis=AX.X,
                apply_absolute_value=True), inc="V", waits=[tEa])
            t_t1 = E("tensor", lambda e: e.transpose(
                ps_pv[0:1, 16:144], smallf[:, 11:12], eyef_sb[:, :]),
                inc="P", waits=[m1, pv[0].wgate()])
            pv[0].wrote(t_t1)
            t_sr = E("vector", lambda e: e.tensor_copy(srow[:, :], ps_pv[0:1, 16:144]),
                     inc="V", waits=[t_t1])
            pv[0].read(t_sr)
            def v8(e_idx):
                return srow[:, :].rearrange("o (h ee) -> o h ee", ee=8)[:, :, e_idx]
            t_mx = t_sr
            for ee in range(1, 8):
                t_mx = E("vector", lambda e, ee=ee: e.tensor_tensor(
                    v8(0), v8(0), v8(ee), op=ALU.max),
                    inc="V", waits=[t_mx])
            t_mc = E("vector", lambda e: e.tensor_scalar(
                v8(0), v8(0), 1e-6, None, op0=ALU.max), inc="V", waits=[t_mx])
            t_cp = E("vector", lambda e: e.tensor_copy(
                srow[:, 0:16], v8(0)), inc="V", waits=[t_mc])
            t_rc = E("vector", lambda e: e.reciprocal(srow[:, 0:16], srow[:, 0:16]),
                     inc="V", waits=[t_cp])
            t_t2 = E("tensor", lambda e: e.transpose(
                ps_pv[0:16, 0:1], srow[:, 0:16], eyef_sb[0:1, 0:1]),
                inc="P", waits=[t_rc, pv[0].wgate()])
            pv[0].wrote(t_t2)
            t_r16 = E("vector", lambda e: e.tensor_copy(rs16[:, :], ps_pv[0:16, 0:1]),
                      inc="V", waits=[t_t2])
            pv[0].read(t_r16)
            t_bc = E("tensor", lambda e: e.matmul(
                ps_pv[:, 4:5], hpat_sb[:, :], rs16[:, :],
                start=True, stop=True, skip_group_check=True),
                inc="P", waits=[t_r16, t_hp, pv[0].wgate()])
            pv[0].wrote(t_bc)
            t_rs128 = E("vector", lambda e: e.tensor_copy(
                smallf[:, 8:9], ps_pv[:, 4:5]), inc="V", waits=[t_bc])
            pv[0].read(t_rs128)
            rscale = smallf[:, 8:9]

            t_tg = E("scalar", lambda e: e.activation(
                gB[:, :], gA[:, :], AF.Sigmoid, scale=10.0,
                bias=constc[:, 1:2]), inc="A", waits=[tEa, t_constc])
            E("vector", lambda e: e.tensor_scalar(
                gB[:, :], gB[:, :], 0.9, 0.1, op0=ALU.mult, op1=ALU.add),
                waits=[t_tg])
            E("vector", lambda e: e.scalar_tensor_tensor(
                gB[:, :], gA[:, :], rscale, gB[:, :],
                op0=ALU.mult, op1=ALU.mult), waits=[t_rs128])
            E("vector", lambda e: e.tensor_scalar(
                gC[:, :], gB[:, :], dtc, None, op0=ALU.mult), waits=[t_kd])
            E("vector", lambda e: e.tensor_scalar(
                gB[:, :], gB[:, :], akd, None, op0=ALU.subtract))
            E("vector", lambda e: e.scalar_tensor_tensor(
                gB[:, :], gC[:, :], negkd, gB[:, :], op0=ALU.mult, op1=ALU.add))
            E("vector", lambda e: e.scalar_tensor_tensor(
                gB[:, :], gC[:, :], 1.0, gB[:, :], op0=ALU.mult, op1=ALU.add))
            E("vector", lambda e: e.tensor_tensor(
                gD[:, :], gC[:, :], gC[:, :], op=ALU.mult))
            E("vector", lambda e: e.tensor_tensor(
                gD[:, :], gD[:, :], gC[:, :], op=ALU.mult))
            E("vector", lambda e: e.scalar_tensor_tensor(
                gB[:, :], gD[:, :], -1.0 / 3.0, gB[:, :],
                op0=ALU.mult, op1=ALU.add))
            t_v2 = E("vector", lambda e: e.scalar_tensor_tensor(
                gC[:, :], gB[:, :], dtc, gC[:, :],
                op0=ALU.mult, op1=ALU.add), inc="V")
            t_sg = E("scalar", lambda e: e.activation(
                gB[:, :], gC[:, :], AF.Sigmoid), inc="A", waits=[t_v2])
            E("vector", lambda e: e.tensor_scalar(
                gB[:, :], gB[:, :], 0.5, 0.5, op0=ALU.mult, op1=ALU.add),
                waits=[t_sg])
            E("vector", lambda e: e.tensor_tensor(
                gD[:, :], gB[:, :], gA[:, :], op=ALU.mult))
            E("vector", lambda e: e.tensor_scalar(
                gD[:, :], gD[:, :], 1e-8, None, op0=ALU.add))
            E("vector", lambda e: e.reciprocal(gD[:, :], gD[:, :]))
            E("vector", lambda e: e.tensor_tensor(
                gB[:, :], gB[:, :], gD[:, :], op=ALU.mult))
            t_s128 = E("vector", lambda e: e.tensor_tensor(
                S128[:, :], gB[:, :], rcl[:, :], op=ALU.mult), inc="V")
            t_sd = E("sync", lambda e: e.dma_start(S_hbm, S128[:, :]),
                     inc="S", waits=[t_s128])
            tok_S = E("sync", lambda e: e.dma_start(
                Smat[:, :], S_hbm.rearrange("(h ee) w -> h (ee w)", ee=8)),
                inc="S", waits=[t_sd])

            xbufs = [xt, xt2]
            xbt = [b_xt, Buf()]
            b_x2 = b_x1t

            def load_c(ti, first):
                bi = ti % 2
                tok = E("sync", lambda e, ti=ti, bi=bi: e.dma_start(
                    xbufs[bi][:, :].rearrange("p (u d) -> p u d", u=4),
                    x1s.rearrange("(tt u p) d -> tt p u d", u=4, p=128)[ti]),
                    inc="S", waits=[xbt[bi].wgate(), tok_AB if first else None])
                xbt[bi].wrote(tok)
                return tok

            sc_toks = {}

            def stage_scale(ti, first=False):
                tok_ri = E("sync", lambda e, ti=ti: e.dma_start(
                    rawin[:, :].rearrange("p (c w) -> p c w", w=512),
                    rawTs.rearrange("(c p) tt -> p c tt", p=128)[:, :, ti * 512:(ti + 1) * 512]),
                    inc="S", waits=[b_rawin.wgate(), tok_AB if first else None])
                b_rawin.wrote(tok_ri)
                t_last = None
                for c in range(8):
                    bank = c % 4
                    t_bc2 = E("tensor", lambda e, c=c, ti=ti, bank=bank: e.matmul(
                        ps_sc[:, bank * 512:(bank + 1) * 512],
                        epat_sb[:, c * 128:(c + 1) * 128],
                        Smat[:, ti * 512:(ti + 1) * 512],
                        start=True, stop=True, skip_group_check=True),
                        inc="P",
                        waits=[tok_S, t_ep, sc[bank].wgate(),
                               tok_woutT if first and c == 0 else None])
                    sc[bank].wrote(t_bc2)
                    t_mul = E("vector", lambda e, c=c, bank=bank: e.tensor_tensor(
                        scaledT[:, c * 512:(c + 1) * 512],
                        rawin[:, c * 512:(c + 1) * 512],
                        ps_sc[:, bank * 512:(bank + 1) * 512], op=ALU.mult),
                        inc="V",
                        waits=[t_bc2, tok_ri, b_scaled.wgate() if c == 0 else None])
                    sc[bank].read(t_mul)
                    b_rawin.read(t_mul)
                    t_last = t_mul
                b_scaled.wrote(t_last)
                sc_toks[ti] = t_last

            tok_x1 = load_c(0, True)
            stage_scale(0, True)
            tok_out_final = None
            for ti in range(NT):
                bi = ti % 2
                cur = xbufs[bi]
                curb = xbt[bi]
                t_sc_last = sc_toks[ti]
                s_toks = []
                for u in range(4):
                    for half in range(2):
                        pvs = half
                        t_mo = None
                        for c in range(8):
                            t_mo = E("tensor", lambda e, c=c, u=u, half=half, pvs=pvs: e.matmul(
                                ps_pv[:, pvs * 512:(pvs + 1) * 512],
                                scaledT[:, c * 512 + u * 128: c * 512 + (u + 1) * 128],
                                woutT[:, c * 1024 + half * 512: c * 1024 + (half + 1) * 512],
                                start=(c == 0), stop=(c == 7), skip_group_check=True),
                                inc="P" if c == 7 else None,
                                waits=[t_sc_last, pv[pvs].wgate()] if c == 0 else [])
                        pv[pvs].wrote(t_mo)
                        b_scaled.read(t_mo)
                        col = u if half == 0 else 12 + u
                        tgt = stats if half == 0 else smallf
                        t_x2 = E("vector", lambda e, u=u, half=half, pvs=pvs,
                                 col=col, tgt=tgt, cur=cur: e.scalar_tensor_tensor(
                            x1t[:, u * 1024 + half * 512: u * 1024 + (half + 1) * 512],
                            ps_pv[:, pvs * 512:(pvs + 1) * 512], 1.0,
                            cur[:, u * 1024 + half * 512: u * 1024 + (half + 1) * 512],
                            op0=ALU.mult, op1=ALU.add,
                            accum_out=tgt[:, col:col + 1]),
                            inc="V",
                            waits=[t_mo, tok_x1,
                                   b_x2.wgate() if (u == 0 and half == 0) else None,
                                   b_stats.wgate() if (u == 0 and half == 0) else None])
                        pv[pvs].read(t_x2)
                        curb.read(t_x2)
                        s_toks.append(t_x2)
                b_x2.wrote(s_toks[-1])
                if ti + 1 < NT:
                    tok_x1 = load_c(ti + 1, False)
                    stage_scale(ti + 1)
                t_cmb = E("vector", lambda e: e.tensor_add(
                    stats[:, 0:4], stats[:, 0:4], smallf[:, 12:16]),
                    inc="V", waits=[s_toks])
                sq3 = []
                for u in range(4):
                    t_sq = E("scalar", lambda e, u=u, cur=cur: e.activation(
                        cur[:, u * 1024:(u + 1) * 1024],
                        x1t[:, u * 1024:(u + 1) * 1024],
                        AF.Square, accum_out=stats[:, 4 + u:5 + u]),
                        inc="A", waits=[s_toks[2 * u + 1],
                                        b_stats.wgate() if u == 0 else None])
                    b_x2.read(t_sq)
                    curb.read(t_sq)
                    sq3.append(t_sq)
                b_stats.wrote(sq3[3])
                tok_st3 = ln_stats_chain(t_cmb, sq3[3])
                tp3 = []
                for u in range(4):
                    slot = u % 2
                    t_ap = E("vector", lambda e, u=u, slot=slot: e.tensor_scalar(
                        lnf32[:, slot * 1024:(slot + 1) * 1024],
                        x1t[:, u * 1024:(u + 1) * 1024],
                        stats[:, 24 + u:25 + u], stats[:, 28 + u:29 + u],
                        op0=ALU.mult, op1=ALU.add),
                        inc="V",
                        waits=[s_toks[-1], tok_st3, b_lnf[slot].wgate()])
                    b_lnf[slot].wrote(t_ap)
                    b_x2.read(t_ap)
                    tps = transpose_subtile(u, tft, b_tft, t_ap, lnf32)
                    for t_ev in tps:
                        b_lnf[slot].read(t_ev)
                    tp3 += tps
                b_tft.wrote(tp3[-1])
                t_x3, _s = ffn_block([tp3, b_tft.w], cur, curb, x1t, b_x2,
                                     s_toks[-1], [8, 9, 10, 11])
                tok_out_final = E("sync", lambda e, ti=ti, cur=cur: e.dma_start(
                    out.rearrange("(tt u p) d -> tt p u d", u=4, p=128)[ti],
                    cur[:, :].rearrange("p (u d) -> p u d", u=4)),
                    inc="S", waits=[t_x3])
                curb.read(tok_out_final)

            E("sync", lambda e: e.nop(), waits=[tok_out_final])

        plan.finish(ctx.enter_context(nc.Block()))
    return nc


_CONSTS = None
_NC_CACHE = {}


def kernel(L0_attn_x, L0_attn_Wqkv, L0_attn_Wout, L0_attn_a, L0_attn_b,
           L0_attn_dt, L0_ffn_Win, L0_ffn_bin_, L0_ffn_W1, L0_ffn_b1,
           L0_ffn_W2, L0_ffn_b2, L0_ffn_Wout, L0_ffn_bout):
    global _CONSTS
    if _CONSTS is None:
        _CONSTS = build_consts()
    consts = _CONSTS

    xf = np.ascontiguousarray(np.asarray(L0_attn_x, dtype=np.float32))
    common = {
        "wqkv": np.ascontiguousarray(np.asarray(L0_attn_Wqkv, dtype=np.float32)),
        "wout": np.ascontiguousarray(np.asarray(L0_attn_Wout, dtype=np.float32)),
        "sa": np.asarray(L0_attn_a, dtype=np.float32).reshape(1),
        "sb": np.asarray(L0_attn_b, dtype=np.float32).reshape(1),
        "sdt": np.asarray(L0_attn_dt, dtype=np.float32).reshape(1),
        "fwin": np.ascontiguousarray(np.asarray(L0_ffn_Win, dtype=np.float32)),
        "fbin": np.asarray(L0_ffn_bin_, dtype=np.float32).reshape(M),
        "fw1": np.asarray(L0_ffn_W1, dtype=np.float32).reshape(M * INNER),
        "fb1": np.asarray(L0_ffn_b1, dtype=np.float32).reshape(M * INNER),
        "fw2": np.asarray(L0_ffn_W2, dtype=np.float32).reshape(M * INNER),
        "fb2": np.asarray(L0_ffn_b2, dtype=np.float32).reshape(M),
        "fwout": np.ascontiguousarray(np.asarray(L0_ffn_Wout, dtype=np.float32)),
        "fbout": np.asarray(L0_ffn_bout, dtype=np.float32).reshape(D),
    }
    common.update(consts)

    nc = build_program(debug=False)
    in_maps = []
    for core in range(8):
        m = dict(common)
        m["x"] = np.ascontiguousarray(xf[core])
        in_maps.append(m)
    res = run_bass_kernel_spmd(nc, in_maps, core_ids=list(range(8)))
    return np.stack([np.asarray(r["out"]) for r in res.results], axis=0)


# version 11
# speedup vs baseline: 1.0075x; 1.0075x over previous
import math
import contextlib
import numpy as np
import ml_dtypes

import concourse.bass as bass
import concourse.mybir as mybir
from concourse.bass_utils import run_bass_kernel_spmd

F32 = mybir.dt.float32
BF16 = mybir.dt.bfloat16
AF = mybir.ActivationFunctionType
ALU = mybir.AluOpType
AX = mybir.AxisListType

T = 4096
D = 1024
H = 16
HD = 64
QT = 512
NT = T // QT
NG = 2
GH = H // NG
M = 64
INNER = 16
EPS = 1e-5
CMARG = math.log(1.04)

ENGINES = ("sync", "gpsimd", "scalar", "vector", "tensor")
SEMS = {
    "S": ("sync", True),
    "G": ("gpsimd", True),
    "P": ("tensor", False),
    "A": ("scalar", False),
    "V": ("vector", False),
    "K": ("gpsimd", False),
}


def _flat(waits):
    for w in waits:
        if w is None:
            continue
        if isinstance(w, list):
            yield from _flat(w)
        else:
            yield w


class Plan:
    def __init__(self, nc, ctx):
        self.nc = nc
        self.sems = {}
        self.cnt = {}
        self.ops = {e: [] for e in ENGINES}
        self.lastw = {e: {} for e in ENGINES}
        for name in SEMS:
            self.sems[name] = ctx.enter_context(nc.semaphore(name))
            self.cnt[name] = 0

    def emit(self, eng, fn, inc=None, waits=()):
        best = {}
        for s, v in _flat(waits):
            if v > best.get(s, 0):
                best[s] = v
        ws = []
        for s, v in best.items():
            if v <= self.lastw[eng].get(s, 0):
                continue
            ws.append((s, v))
            self.lastw[eng][s] = v
        tok_out = None
        inc_rec = None
        if inc:
            amt = 16 if SEMS[inc][1] else 1
            self.cnt[inc] += amt
            tok_out = (inc, self.cnt[inc])
            inc_rec = (inc, amt)
        self.ops[eng].append((tuple(ws), fn, inc_rec))
        return tok_out

    def finish(self, blk):
        sems = self.sems
        for eng_name in ENGINES:
            ops = self.ops[eng_name]

            def make_body(ops):
                def body(eng):
                    for ws, fn, inc in ops:
                        for s, v in ws:
                            eng.wait_ge(sems[s], v)
                        instr = fn(eng)
                        if inc is not None:
                            instr.then_inc(sems[inc[0]], inc[1])
                return body
            getattr(blk, eng_name)(make_body(ops))


class Buf:
    __slots__ = ("w", "r")

    def __init__(self):
        self.w = None
        self.r = []

    def wgate(self):
        return self.r + [self.w]

    def wrote(self, tok):
        if tok is not None:
            self.w = tok
            self.r = []

    def read(self, tok):
        if tok is not None:
            self.r.append(tok)


def build_consts():
    c = {}
    c["eyebf"] = np.eye(128, dtype=ml_dtypes.bfloat16)
    c["eyef"] = np.eye(128, dtype=np.float32)
    m = np.zeros((128, 4, QT), dtype=np.float32)
    for k in range(4):
        for s in range(128):
            m[s, k, :] = (np.arange(QT) >= (k * 128 + s)).astype(np.float32)
    c["masks"] = m.reshape(128, 4 * QT).astype(ml_dtypes.bfloat16)
    e = np.zeros((128, 128), dtype=np.float32)
    e[0:64, 0] = 1.0
    e[64:128, 64] = 1.0
    c["eab"] = e.astype(ml_dtypes.bfloat16)
    ep = np.zeros((16, 8, 128), dtype=np.float32)
    for ch in range(8):
        ep[2 * ch, ch, 0:64] = 1.0
        ep[2 * ch + 1, ch, 64:128] = 1.0
    c["epat"] = ep.reshape(16, 8 * 128)
    hp = np.zeros((16, 128), dtype=np.float32)
    for h in range(16):
        hp[h, h * 8:(h + 1) * 8] = 1.0
    c["hpat"] = hp
    sp = np.zeros((128, 8, 64), dtype=np.float32)
    st = np.zeros((64, 8, 128), dtype=np.float32)
    for ch in range(8):
        for p in range(128):
            mm = (128 * ch + p) // 16
            sp[p, ch, mm] = 1.0
            st[mm, ch, p] = 1.0
    c["selpat"] = sp.reshape(128, 8 * 64).astype(ml_dtypes.bfloat16)
    c["selpatT"] = st.reshape(64, 8 * 128).astype(ml_dtypes.bfloat16)
    eb = np.zeros((64, 1024), dtype=np.float32)
    for flat in range(1024):
        eb[flat // 16, flat] = 1.0
    c["ebf"] = eb.astype(ml_dtypes.bfloat16)
    return c


def build_program(debug=False):
    nc = bass.Bass("TRN2", target_bir_lowering=False)

    x_in = nc.dram_tensor("x", [T, D], F32, kind="ExternalInput").ap()
    wqkv = nc.dram_tensor("wqkv", [3 * D, D], F32, kind="ExternalInput").ap()
    wout = nc.dram_tensor("wout", [D, D], F32, kind="ExternalInput").ap()
    s_a = nc.dram_tensor("sa", [1], F32, kind="ExternalInput").ap()
    s_b = nc.dram_tensor("sb", [1], F32, kind="ExternalInput").ap()
    s_dt = nc.dram_tensor("sdt", [1], F32, kind="ExternalInput").ap()
    fwin = nc.dram_tensor("fwin", [D, M], F32, kind="ExternalInput").ap()
    fbin = nc.dram_tensor("fbin", [M], F32, kind="ExternalInput").ap()
    fw1 = nc.dram_tensor("fw1", [M * INNER], F32, kind="ExternalInput").ap()
    fb1 = nc.dram_tensor("fb1", [M * INNER], F32, kind="ExternalInput").ap()
    fw2 = nc.dram_tensor("fw2", [M * INNER], F32, kind="ExternalInput").ap()
    fb2 = nc.dram_tensor("fb2", [M], F32, kind="ExternalInput").ap()
    fwout = nc.dram_tensor("fwout", [M, D], F32, kind="ExternalInput").ap()
    fbout = nc.dram_tensor("fbout", [D], F32, kind="ExternalInput").ap()
    c_eye = nc.dram_tensor("eyebf", [128, 128], BF16, kind="ExternalInput").ap()
    c_eyef = nc.dram_tensor("eyef", [128, 128], F32, kind="ExternalInput").ap()
    c_masks = nc.dram_tensor("masks", [128, 4 * QT], BF16, kind="ExternalInput").ap()
    c_eab = nc.dram_tensor("eab", [128, 128], BF16, kind="ExternalInput").ap()
    c_epat = nc.dram_tensor("epat", [16, 8 * 128], F32, kind="ExternalInput").ap()
    c_hpat = nc.dram_tensor("hpat", [16, 128], F32, kind="ExternalInput").ap()
    c_selpat = nc.dram_tensor("selpat", [128, 8 * 64], BF16, kind="ExternalInput").ap()
    c_selpatT = nc.dram_tensor("selpatT", [64, 8 * 128], BF16, kind="ExternalInput").ap()
    c_ebf = nc.dram_tensor("ebf", [64, 1024], BF16, kind="ExternalInput").ap()

    out = nc.dram_tensor("out", [T, D], F32, kind="ExternalOutput").ap()

    skind = "ExternalOutput" if debug else "Internal"
    U16 = mybir.dt.uint16
    x1s = nc.dram_tensor("x1s", [T, D], F32, kind=skind).ap()
    y1s = nc.dram_tensor("y1s", [T, D], U16 if debug else BF16, kind=skind).ap()
    if debug:
        y1s = y1s.bitcast(BF16)
    rawTs = nc.dram_tensor("rawTs", [D, T], U16 if debug else BF16, kind=skind).ap()
    if debug:
        rawTs = rawTs.bitcast(BF16)
    Ls = nc.dram_tensor("Ls", [H, T], F32, kind=skind).ap()
    dbg1 = nc.dram_tensor("dbg1", [128, 32], F32, kind=skind).ap() if debug else None
    dbg2 = nc.dram_tensor("dbg2", [128, 1048], F32, kind=skind).ap() if debug else None
    dbg3 = nc.dram_tensor("dbg3", [128, 1664], mybir.dt.uint16, kind=skind).ap() if debug else None
    wqkv_bf = nc.dram_tensor("wqkv_bf", [3 * D, D], BF16, kind="Internal").ap()
    wout_bf = nc.dram_tensor("wout_bf", [D, D], BF16, kind="Internal").ap()
    S_hbm = nc.dram_tensor("S_hbm", [128, 512], F32, kind="Internal").ap()
    t1s = nc.dram_tensor("t1s", [T, D], BF16, kind="Internal").ap()

    with contextlib.ExitStack() as ctx:
        def sbuf(name, shape, dt, stack=None):
            return (stack or ctx).enter_context(nc.sbuf_tensor(name, shape, dt))

        wqkvT = sbuf("wqkvT", [128, 8 * 3072], BF16)
        win = sbuf("win", [128, 8 * 64], BF16)
        woutf = sbuf("woutf", [64, 1024], BF16)
        ew1 = sbuf("ew1", [64, 8 * 128], BF16)
        bdw2 = sbuf("bdw2", [128, 8 * 64], BF16)
        b1p = sbuf("b1p", [128, 8], F32)
        hbbc = sbuf("hbbc", [128, 1024], F32)
        eab_sb = sbuf("eab_sb", [128, 128], BF16)
        eye_sb = sbuf("eye_sb", [128, 128], BF16)
        ones_b = sbuf("ones_b", [1, 128], BF16)
        ones_f = sbuf("ones_f", [128, 128], F32)
        xt = sbuf("xt", [128, 4 * 1024], F32)
        x1t = sbuf("x1t", [128, 4 * 1024], F32)
        lntok = sbuf("lntok", [128, 2 * 1024], BF16)
        tft = sbuf("tft", [128, 8 * 512], BF16)
        stats = sbuf("stats", [128, 32], F32)
        sbf = sbuf("sbf", [64, 512], BF16)
        gbf = sbuf("gbf", [64, 512], BF16)
        hbf = sbuf("hbf", [128, 2 * 512], BF16)
        qsqbf = sbuf("qsqbf", [128, 512], BF16)
        smallf = sbuf("smallf", [128, 16], F32)
        w1col = sbuf("w1col", [128, 8], F32)
        b1col = sbuf("b1col", [128, 8], F32)
        w2col = sbuf("w2col", [128, 8], F32)
        binexp = sbuf("binexp", [128, 8], F32)
        svals = sbuf("svals", [1, 4], F32)
        constc = sbuf("constc", [128, 2], F32)
        eyef_sb = sbuf("eyef_sb", [128, 128], F32)

        ps_sc = ctx.enter_context(nc.psum_tensor("ps_sc", [128, 3072], F32))
        ps_pv = ctx.enter_context(nc.psum_tensor("ps_pv", [128, 1024], F32))

        plan = Plan(nc, ctx)
        E = plan.emit

        sc = [Buf() for _ in range(6)]
        pv = [Buf() for _ in range(2)]
        b_xt, b_x1t, b_tft = Buf(), Buf(), Buf()
        b_lnslot = [Buf(), Buf()]
        b_hbf = [Buf(), Buf()]
        b_sbf, b_gbf, b_qsq, b_stats = Buf(), Buf(), Buf(), Buf()

        init = contextlib.ExitStack()
        with init:
            rowbuf = sbuf("rowbuf", [1, 1024], F32, init)
            rowbuf2 = sbuf("rowbuf2", [1, 1024], F32, init)
            w1row = sbuf("w1row", [1, 1024], BF16, init)
            b2colb = sbuf("b2colb", [64, 1], BF16, init)
            binb = sbuf("binb", [64, 1], BF16, init)
            selpat_sb = sbuf("selpat_sb", [128, 8 * 64], BF16, init)
            w1row8 = sbuf("w1row8", [8, 128], F32, init)
            b1row8 = sbuf("b1row8", [8, 128], F32, init)
            w2row8 = sbuf("w2row8", [8, 128], F32, init)
            selpatT_sb = sbuf("selpatT_sb", [64, 8 * 128], BF16, init)
            ebf_sb = sbuf("ebf_sb", [64, 1024], BF16, init)

            tok_eye = E("sync", lambda e: e.dma_start(eye_sb[:, :], c_eye), inc="S")
            tok_eyef = E("sync", lambda e: e.dma_start(eyef_sb[:, :], c_eyef), inc="S")
            tok_eab = E("sync", lambda e: e.dma_start(eab_sb[:, :], c_eab), inc="S")
            t_sp = E("sync", lambda e: e.dma_start(selpat_sb[:, :], c_selpat), inc="S")
            t_spt = E("sync", lambda e: e.dma_start(selpatT_sb[:, :], c_selpatT), inc="S")
            t_ebf = E("sync", lambda e: e.dma_start(ebf_sb[:, :], c_ebf), inc="S")

            E("gpsimd", lambda e: e.memset(ones_b[:, :], 1.0), inc="K")
            t_ones = E("gpsimd", lambda e: e.memset(ones_f[:, :], 1.0), inc="K")
            E("vector", lambda e: e.memset(smallf[:, :], 1.0))
            E("vector", lambda e: e.memset(stats[:, :], 1.0))
            E("vector", lambda e: e.memset(constc[:, 0:1], CMARG))
            t_constc = E("vector", lambda e: e.memset(constc[:, 1:2], -5.0), inc="V")

            t_wq = E("gpsimd", lambda e: e.dma_start(wqkv_bf, wqkv), inc="G")
            t_wo = E("gpsimd", lambda e: e.dma_start(wout_bf, wout), inc="G")
            t_win = E("gpsimd", lambda e: e.dma_start(
                win[:, :].rearrange("k (c m) -> k c m", m=M),
                fwin.rearrange("(c k) m -> k c m", k=128)), inc="G")
            t_wfo = E("gpsimd", lambda e: e.dma_start(woutf[:, :], fwout), inc="G")
            t_w1r8 = E("gpsimd", lambda e: e.dma_start(
                w1row8[:, :], fw1.rearrange("(c p) -> c p", p=128)), inc="G")
            t_b1r8 = E("gpsimd", lambda e: e.dma_start(
                b1row8[:, :], fb1.rearrange("(c p) -> c p", p=128)), inc="G")
            t_w2r8 = E("gpsimd", lambda e: e.dma_start(
                w2row8[:, :], fw2.rearrange("(c p) -> c p", p=128)), inc="G")
            t_tw1 = E("tensor", lambda e: e.transpose(
                ps_sc[:, 0:8], w1row8[:, :], eyef_sb[0:8, 0:8]),
                waits=[t_w1r8, tok_eyef])
            t_tb1 = E("tensor", lambda e: e.transpose(
                ps_sc[:, 8:16], b1row8[:, :], eyef_sb[0:8, 0:8]), waits=[t_b1r8])
            t_tw2 = E("tensor", lambda e: e.transpose(
                ps_sc[:, 16:24], w2row8[:, :], eyef_sb[0:8, 0:8]), inc="P",
                waits=[t_w2r8])
            t_w1c = E("vector", lambda e: e.tensor_copy(w1col[:, :], ps_sc[:, 0:8]),
                      inc="V", waits=[t_tw2])
            t_b1c = E("vector", lambda e: e.tensor_copy(b1col[:, :], ps_sc[:, 8:16]),
                      inc="V")
            t_w2c = E("vector", lambda e: e.tensor_copy(w2col[:, :], ps_sc[:, 16:24]),
                      inc="V")
            sc[0].wrote(t_tw2)
            sc[0].read(t_w2c)
            t_w1r = E("gpsimd", lambda e: e.dma_start(
                w1row[:, :], fw1.rearrange("(one f) -> one f", one=1)), inc="G")
            t_b2b = E("gpsimd", lambda e: e.dma_start(
                b2colb[:, :], fb2.rearrange("(p one) -> p one", one=1)), inc="G")
            t_binb = E("gpsimd", lambda e: e.dma_start(
                binb[:, :], fbin.rearrange("(p one) -> p one", one=1)), inc="G")
            t_bout = E("gpsimd", lambda e: e.dma_start(
                rowbuf2[:, :], fbout.rearrange("(one f) -> one f", one=1)), inc="G")
            E("gpsimd", lambda e: e.dma_start(
                svals[:, 0:1], s_a.rearrange("(o f) -> o f", o=1)), inc="G")
            E("gpsimd", lambda e: e.dma_start(
                svals[:, 1:2], s_b.rearrange("(o f) -> o f", o=1)), inc="G")
            t_svals = E("gpsimd", lambda e: e.dma_start(
                svals[:, 2:3], s_dt.rearrange("(o f) -> o f", o=1)), inc="G")

            tok_wqkvT = None
            for c in range(8):
                tok_wqkvT = E("sync", lambda e, c=c: e.dma_start(
                    wqkvT[:, c * 3072:(c + 1) * 3072],
                    wqkv_bf[:, c * 128:(c + 1) * 128], transpose=True),
                    inc="S", waits=[t_wq])

            tok_bd = None
            for c in range(8):
                tok_bd = E("vector", lambda e, c=c: e.tensor_scalar(
                    bdw2[:, c * 64:(c + 1) * 64], selpat_sb[:, c * 64:(c + 1) * 64],
                    w2col[:, c:c + 1], None, op0=ALU.mult),
                    inc="V" if c == 7 else None, waits=[t_sp, t_w2c])

            E("tensor", lambda e: e.matmul(
                ps_sc[0:64, 0:512], ones_b[:, 0:64], w1row[:, 0:512],
                start=True, stop=True),
                waits=[t_w1r, t_ones] + sc[0].wgate())
            t_mmw1 = E("tensor", lambda e: e.matmul(
                ps_sc[0:64, 512:1024], ones_b[:, 0:64], w1row[:, 512:1024],
                start=True, stop=True), inc="P")
            sc[0].wrote(t_mmw1)
            sc[1].wrote(t_mmw1)
            t_ew1 = E("vector", lambda e: e.tensor_tensor(
                ew1[:, :], ps_sc[0:64, 0:1024], selpatT_sb[:, :], op=ALU.mult),
                inc="V", waits=[t_mmw1, t_spt])
            sc[0].read(t_ew1)
            sc[1].read(t_ew1)

            t_mmbe = None
            for c in range(8):
                t_mmbe = E("tensor", lambda e, c=c: e.matmul(
                    ps_pv[:, c:c + 1], ebf_sb[:, c * 128:(c + 1) * 128], binb[:, :],
                    start=True, stop=True),
                    inc="P" if c == 7 else None, waits=[t_ebf, t_binb])
            pv[0].wrote(t_mmbe)
            t_bex = E("vector", lambda e: e.tensor_copy(binexp[:, :], ps_pv[:, 0:8]),
                      inc="V", waits=[t_mmbe])
            pv[0].read(t_bex)
            t_b1a = E("vector", lambda e: e.tensor_tensor(
                b1p[:, :], w1col[:, :], binexp[:, :], op=ALU.mult), inc="V",
                waits=[t_w1c, t_bex])
            t_b1p = E("vector", lambda e: e.tensor_tensor(
                b1p[:, :], b1p[:, :], b1col[:, :], op=ALU.add), inc="V",
                waits=[t_b1c, t_b1a])

            E("tensor", lambda e: e.matmul(
                ps_pv[0:1, 512:768], b2colb[:, :], woutf[:, 0:256],
                start=True, stop=True), waits=[t_b2b, t_wfo])
            E("tensor", lambda e: e.matmul(
                ps_pv[0:1, 768:1024], b2colb[:, :], woutf[:, 256:512],
                start=True, stop=True))
            E("tensor", lambda e: e.matmul(
                ps_pv[0:1, 0:256], b2colb[:, :], woutf[:, 512:768],
                start=True, stop=True), waits=pv[0].wgate())
            t_mmb2 = E("tensor", lambda e: e.matmul(
                ps_pv[0:1, 256:512], b2colb[:, :], woutf[:, 768:1024],
                start=True, stop=True), inc="P")
            pv[0].wrote(t_mmb2)
            pv[1].wrote(t_mmb2)
            E("vector", lambda e: e.scalar_tensor_tensor(
                rowbuf[:, 0:512], ps_pv[0:1, 512:1024], 0.5, rowbuf2[:, 0:512],
                op0=ALU.mult, op1=ALU.bypass), waits=[t_mmb2, t_bout])
            t_hb0 = E("vector", lambda e: e.scalar_tensor_tensor(
                rowbuf[:, 512:1024], ps_pv[0:1, 0:512], 0.5, rowbuf2[:, 512:1024],
                op0=ALU.mult, op1=ALU.bypass), inc="V")
            pv[0].read(t_hb0)
            pv[1].read(t_hb0)
            t_hb = E("vector", lambda e: e.scalar_tensor_tensor(
                rowbuf[:, :], rowbuf2[:, :], 0.5, rowbuf[:, :],
                op0=ALU.mult, op1=ALU.add), inc="V")
            t_mmhb = None
            for half in range(2):
                t_mmhb = E("tensor", lambda e, half=half: e.matmul(
                    ps_sc[:, 1024 + half * 512:1536 + half * 512],
                    ones_f[0:1, :], rowbuf[:, half * 512:(half + 1) * 512],
                    start=True, stop=True),
                    inc="P" if half == 1 else None,
                    waits=[t_hb] + sc[2].wgate() + sc[3].wgate())
            sc[2].wrote(t_mmhb)
            sc[3].wrote(t_mmhb)
            t_hbbc = E("vector", lambda e: e.tensor_copy(
                hbbc[:, :], ps_sc[:, 1024:2048]), inc="V", waits=[t_mmhb])
            sc[2].read(t_hbbc)
            sc[3].read(t_hbbc)

            tok_init = [t_b1p, t_hbbc, tok_bd, t_win, t_ew1, tok_wqkvT,
                        t_mmbe, tok_eab, tok_eye, t_svals]
            if debug:
                E("sync", lambda e: e.dma_start(dbg2[:, 0:1024], hbbc[:, :]),
                  inc="S", waits=[t_hbbc])
                E("sync", lambda e: e.dma_start(dbg2[:, 1024:1032], b1p[:, :]),
                  inc="S", waits=[t_b1p])
                E("sync", lambda e: e.dma_start(dbg2[:, 1032:1040], binexp[:, :]),
                  inc="S", waits=[t_bex])
                E("sync", lambda e: e.dma_start(dbg2[:, 1040:1048], w2col[:, :]),
                  inc="S", waits=[t_w2c])
                E("sync", lambda e: e.dma_start(
                    dbg3[:, 0:512], bdw2[:, :].bitcast(mybir.dt.uint16)),
                  inc="S", waits=[tok_bd])
                E("sync", lambda e: e.dma_start(
                    dbg3[0:64, 512:1536], ew1[:, :].bitcast(mybir.dt.uint16)),
                  inc="S", waits=[t_ew1])
                E("sync", lambda e: e.dma_start(
                    dbg3[:, 1536:1600], win[:, 0:64].bitcast(mybir.dt.uint16)),
                  inc="S", waits=[t_win])

        def ln_stats_chain(tok_sums, tok_sqs):
            t1 = E("vector", lambda e: e.tensor_scalar(
                stats[:, 8:12], stats[:, 0:4], 1.0 / D, None, op0=ALU.mult),
                inc="V", waits=[tok_sums])
            t2 = E("vector", lambda e: e.tensor_scalar(
                stats[:, 12:16], stats[:, 4:8], 1.0 / D, EPS,
                op0=ALU.mult, op1=ALU.add), inc="V", waits=[tok_sqs])
            t3 = E("vector", lambda e: e.scalar_tensor_tensor(
                stats[:, 16:20], stats[:, 8:12], 1.0, stats[:, 8:12],
                op0=ALU.mult, op1=ALU.mult), inc="V", waits=[t1])
            t_v = E("vector", lambda e: e.tensor_sub(
                stats[:, 20:24], stats[:, 12:16], stats[:, 16:20]), inc="V",
                waits=[t2, t3])
            b_stats.read(t_v)
            t_ln = E("scalar", lambda e: e.activation(
                stats[:, 20:24], stats[:, 20:24], AF.Ln), inc="A", waits=[t_v])
            t_rs = E("scalar", lambda e: e.activation(
                stats[:, 24:28], stats[:, 20:24], AF.Exp, scale=-0.5), inc="A",
                waits=[t_ln])
            t_nm = E("vector", lambda e: e.scalar_tensor_tensor(
                stats[:, 28:32], stats[:, 8:12], -1.0, stats[:, 24:28],
                op0=ALU.mult, op1=ALU.mult), inc="V", waits=[t_rs, t1])
            return [t_rs, t_nm]

        def ln_apply(u, src_buf, tok_src, tok_stats):
            slot = u % 2
            tok = E("vector", lambda e, u=u, slot=slot: e.tensor_scalar(
                lntok[:, slot * 1024:(slot + 1) * 1024],
                src_buf[:, u * 1024:(u + 1) * 1024],
                stats[:, 24 + u:25 + u], stats[:, 28 + u:29 + u],
                op0=ALU.mult, op1=ALU.add),
                inc="V",
                waits=[tok_src, tok_stats, b_lnslot[slot].wgate()])
            b_lnslot[slot].wrote(tok)
            return tok

        def transpose_subtile(u, dst, dst_buf, tok_slot, srcbuf):
            slot = u % 2
            toks = []
            for half in range(2):
                bank = 4 + half
                t_tp = None
                for cc in range(4):
                    c = half * 4 + cc
                    t_tp = E("tensor", lambda e, c=c, cc=cc, slot=slot, half=half, bank=bank: e.transpose(
                        ps_sc[:, bank * 512 + cc * 128: bank * 512 + (cc + 1) * 128],
                        srcbuf[:, slot * 1024 + c * 128: slot * 1024 + (c + 1) * 128],
                        eyef_sb[:, :]),
                        inc="P" if cc == 3 else None,
                        waits=[tok_slot, sc[bank].wgate()] if cc == 0 else [])
                sc[bank].wrote(t_tp)
                t_ev = E("vector", lambda e, u=u, half=half, bank=bank: e.tensor_copy(
                    dst[:, :].rearrange("p (c w) -> p c w", w=512)
                       [:, half * 4:(half + 1) * 4, u * 128:(u + 1) * 128],
                    ps_sc[:, bank * 512:bank * 512 + 512]
                        .rearrange("p (c w) -> p c w", w=128)),
                    inc="V", waits=[t_tp, dst_buf.wgate() if u == 0 and half == 0 else None])
                sc[bank].read(t_ev)
                toks.append(t_ev)
            return toks

        def ffn_block(tok_tft, dst_buf, dst_tok_buf, src_buf, src_tok_buf, tok_src, sum_cols):
            t_mm = None
            for c in range(8):
                t_mm = E("tensor", lambda e, c=c: e.matmul(
                    ps_sc[0:64, 0:512], win[:, c * 64:(c + 1) * 64],
                    tft[:, c * 512:(c + 1) * 512],
                    start=(c == 0), stop=(c == 7), skip_group_check=True),
                    inc="P" if c == 7 else None,
                    waits=[tok_tft, sc[0].wgate(), tok_init] if c == 0 else [])
            sc[0].wrote(t_mm)
            b_tft.read(t_mm)
            t_sb = E("vector", lambda e: e.tensor_copy(sbf[:, :], ps_sc[0:64, 0:512]),
                     inc="V", waits=[t_mm, b_sbf.wgate()])
            sc[0].read(t_sb)
            b_sbf.wrote(t_sb)
            t_g = None
            for c in range(8):
                bank = 1 + (c % 3)
                t_h = E("tensor", lambda e, c=c, bank=bank: e.matmul(
                    ps_sc[:, bank * 512:(bank + 1) * 512],
                    ew1[:, c * 128:(c + 1) * 128], sbf[:, :],
                    start=True, stop=True, skip_group_check=True),
                    inc="P", waits=[t_sb, sc[bank].wgate()])
                sc[bank].wrote(t_h)
                b_sbf.read(t_h)
                t_r = E("scalar", lambda e, c=c, bank=bank: e.activation(
                    hbf[:, (c % 2) * 512:(c % 2 + 1) * 512],
                    ps_sc[:, bank * 512:(bank + 1) * 512], AF.Relu,
                    bias=b1p[:, c:c + 1]),
                    inc="A", waits=[t_h, b_hbf[c % 2].wgate()])
                sc[bank].read(t_r)
                b_hbf[c % 2].wrote(t_r)
                t_g = E("tensor", lambda e, c=c: e.matmul(
                    ps_sc[0:64, 0:512], bdw2[:, c * 64:(c + 1) * 64],
                    hbf[:, (c % 2) * 512:(c % 2 + 1) * 512],
                    start=(c == 0), stop=(c == 7), skip_group_check=True),
                    inc="P" if c == 7 else None,
                    waits=[t_r, sc[0].wgate() if c == 0 else None])
                b_hbf[c % 2].read(t_g)
            sc[0].wrote(t_g)
            t_gb = E("vector", lambda e: e.tensor_copy(gbf[:, :], ps_sc[0:64, 0:512]),
                     inc="V", waits=[t_g, b_gbf.wgate()])
            sc[0].read(t_gb)
            b_gbf.wrote(t_gb)
            sum_toks = [None] * 4
            for u in range(4):
                t_q = None
                for half in range(2):
                    bank = 4 + half
                    t_mo = E("tensor", lambda e, u=u, half=half, bank=bank: e.matmul(
                        ps_sc[:, bank * 512:(bank + 1) * 512],
                        gbf[:, u * 128:(u + 1) * 128],
                        woutf[:, half * 512:(half + 1) * 512],
                        start=True, stop=True, skip_group_check=True),
                        inc="P", waits=[t_gb, sc[bank].wgate()])
                    sc[bank].wrote(t_mo)
                    b_gbf.read(t_mo)
                    t_q = E("vector", lambda e, u=u, half=half, bank=bank: e.scalar_tensor_tensor(
                        dst_buf[:, u * 1024 + half * 512: u * 1024 + (half + 1) * 512],
                        ps_sc[:, bank * 512:(bank + 1) * 512], 0.5,
                        src_buf[:, u * 1024 + half * 512: u * 1024 + (half + 1) * 512],
                        op0=ALU.mult, op1=ALU.add),
                        inc="V",
                        waits=[t_mo, tok_src, tok_init if u == 0 and half == 0 else None,
                               dst_tok_buf.wgate() if u == 0 and half == 0 else None])
                    sc[bank].read(t_q)
                    src_tok_buf.read(t_q)
                col = sum_cols[u]
                t_s = E("vector", lambda e, u=u, col=col: e.scalar_tensor_tensor(
                    dst_buf[:, u * 1024:(u + 1) * 1024],
                    dst_buf[:, u * 1024:(u + 1) * 1024], 1.0, hbbc[:, :],
                    op0=ALU.mult, op1=ALU.add,
                    accum_out=stats[:, col:col + 1]),
                    inc="V", waits=[t_q, b_stats.wgate() if u == 0 else None])
                sum_toks[u] = t_s
            dst_tok_buf.wrote(sum_toks[3])
            b_stats.wrote(sum_toks[3])
            return sum_toks[3], sum_toks

        ab = contextlib.ExitStack()
        with ab:
            kT = sbuf("kT", [128, 4 * T], BF16, ab)
            vv = sbuf("vv", [128, GH * 32 * 65], BF16, ab)
            qt = sbuf("qt", [128, 4 * 512], BF16, ab)
            pt = sbuf("pt", [128, 6144], BF16, ab)
            y1t = sbuf("y1t", [128, 8 * 512], BF16, ab)
            masks_sb = sbuf("masks_sb", [128, 4 * QT], BF16, ab)
            rawstage = sbuf("rawstage", [128, 4 * 512], BF16, ab)
            lrow = sbuf("lrow", [1, 512], F32, ab)
            cb = sbuf("cb", [128, 8], F32, ab)
            qmaxsc = sbuf("qmaxsc", [128, 4], F32, ab)
            kmaxsc = sbuf("kmaxsc", [128, 4], F32, ab)
            mk2 = sbuf("mk2", [128, 4], F32, ab)
            negc = sbuf("negc", [128, 4], F32, ab)

            b_qt, b_pt, b_y1t, b_raw, b_cb = Buf(), Buf(), Buf(), Buf(), Buf()
            b_lrow = [Buf(), Buf()]
            b_kv = Buf()
            b_cmax = Buf()
            lri = [0]

            tok_masks = E("sync", lambda e: e.dma_start(masks_sb[:, :], c_masks),
                          inc="S", waits=[tok_init])
            E("vector", lambda e: e.memset(qmaxsc[:, :], 1.0), waits=[tok_init])
            E("vector", lambda e: e.memset(kmaxsc[:, :], 1.0))
            E("vector", lambda e: e.memset(negc[:, :], 1.0))
            E("vector", lambda e: e.memset(cb[:, :], 0.0))

            def group_pass(g, tok_start):
                t_mk = E("vector", lambda e: e.memset(mk2[:, :], 0.0), inc="V",
                         waits=[tok_start])
                t_vv = E("gpsimd", lambda e: e.memset(vv[:, :], 1.0), inc="K",
                         waits=[tok_start, b_kv.wgate()])
                b_kv.wrote(t_vv)
                b_cmax.wrote(t_mk)
                tok_tile_end = None
                tok_pv = [[None, None], [None, None]]
                tok_maskw = [[None, None], [None, None]]

                def load_x(ti, tok_first):
                    tok = E("sync", lambda e, ti=ti: e.dma_start(
                        xt[:, :].rearrange("p (u d) -> p u d", u=4),
                        x_in.rearrange("(tt u p) d -> tt p u d", u=4, p=128)[ti]),
                        inc="S", waits=[b_xt.wgate(), tok_first])
                    b_xt.wrote(tok)
                    return tok

                def ln1_stats(ti, tok_x):
                    su_toks, sq_toks = [], []
                    for u in range(4):
                        t_su = E("vector", lambda e, u=u: e.reduce_sum(
                            stats[:, u:u + 1], xt[:, u * 1024:(u + 1) * 1024],
                            axis=AX.X), inc="V",
                            waits=[tok_x, b_stats.wgate() if u == 0 else None])
                        t_sq = E("scalar", lambda e, u=u: e.activation(
                            x1t[:, u * 1024:(u + 1) * 1024],
                            xt[:, u * 1024:(u + 1) * 1024],
                            AF.Square, accum_out=stats[:, 4 + u:5 + u]),
                            inc="A", waits=[tok_x, b_x1t.wgate() if u == 0 else None,
                                            b_stats.wgate() if u == 0 else None])
                        b_x1t.read(t_sq)
                        b_xt.read(t_su)
                        b_xt.read(t_sq)
                        su_toks.append(t_su)
                        sq_toks.append(t_sq)
                    b_stats.wrote(sq_toks[3])
                    return ln_stats_chain(su_toks[3], sq_toks[3])

                def emit_attention(a_ti, a_kv, a_qt, a_cb, mid_emit=None):
                    raw_toks = []
                    nCH = 4 * (a_ti + 1)
                    wins = [list(range(s, min(s + 3, nCH))) for s in range(0, nCH, 3)]
                    nW = len(wins)
                    for ph in range(4):
                        pp = ph
                        tok_expw = [[None, None], [None, None]]
                        tok_pv_stop = [None, None]

                        def emit_qk(w, hh):
                            t_qk = None
                            for ji, j in enumerate(wins[w]):
                                bank = 3 * hh + j % 3
                                t_qk = E("tensor", lambda e, j=j, pp=pp, hh=hh, bank=bank: e.matmul(
                                    ps_sc[:, bank * 512:(bank + 1) * 512],
                                    kT[hh * 64:hh * 64 + 64,
                                       pp * T + j * 128: pp * T + (j + 1) * 128],
                                    qt[hh * 64:hh * 64 + 64,
                                       pp * 512:(pp + 1) * 512],
                                    start=True, stop=True, skip_group_check=True),
                                    inc="P" if ji == len(wins[w]) - 1 else None,
                                    waits=[sc[3 * hh + (j + ji0) % 3].wgate()
                                           for ji0 in range(len(wins[w]))] +
                                          [a_kv, a_qt] if ji == 0 else [])
                                if ji == len(wins[w]) - 1:
                                    for jj in wins[w]:
                                        sc[3 * hh + jj % 3].wrote(t_qk)
                                    b_qt.read(t_qk)
                                    b_kv.read(t_qk)
                            return t_qk

                        def emit_exp(w, hh, t_qk):
                            lh = 2 * pp + hh
                            slot = w % 2
                            j0 = wins[w][0]
                            nwc = len(wins[w])
                            b0 = 3 * hh + j0 % 3
                            t_ex = E("scalar", lambda e, hh=hh, lh=lh, slot=slot, b0=b0, nwc=nwc: e.activation(
                                pt[:, hh * 3072 + slot * 1536: hh * 3072 + slot * 1536 + nwc * 512],
                                ps_sc[:, b0 * 512:(b0 + nwc) * 512], AF.Exp,
                                bias=cb[:, lh:lh + 1]),
                                inc="A",
                                waits=[t_qk, a_cb, tok_pv[hh][slot],
                                       tok_maskw[hh][slot]])
                            for jj in wins[w]:
                                sc[3 * hh + jj % 3].read(t_ex)
                            b_cb.read(t_ex)
                            tok_expw[hh][slot] = t_ex
                            if wins[w][-1] >= nCH - 4:
                                t_m = None
                                first = True
                                for ji, j in enumerate(wins[w]):
                                    k = j - (nCH - 4)
                                    if k < 0:
                                        continue
                                    t_m = E("gpsimd", lambda e, hh=hh, slot=slot, ji=ji, k=k: e.tensor_tensor(
                                        pt[:, hh * 3072 + slot * 1536 + ji * 512:
                                           hh * 3072 + slot * 1536 + (ji + 1) * 512],
                                        pt[:, hh * 3072 + slot * 1536 + ji * 512:
                                           hh * 3072 + slot * 1536 + (ji + 1) * 512],
                                        masks_sb[:, k * 512:(k + 1) * 512],
                                        op=ALU.mult),
                                        inc="K",
                                        waits=[t_ex, tok_masks] if first else [])
                                    first = False
                                if t_m is not None:
                                    tok_maskw[hh][slot] = t_m
                                    tok_expw[hh][slot] = t_m
                            return tok_expw[hh][slot]

                        def emit_pv(w, hh):
                            lh = 2 * pp + hh
                            slot = w % 2
                            t_pv = None
                            for ji, j in enumerate(wins[w]):
                                is_start = (w == 0 and ji == 0)
                                is_stop = (w == nW - 1 and ji == len(wins[w]) - 1)
                                t_pv = E("tensor", lambda e, ji=ji, j=j, lh=lh, hh=hh, slot=slot,
                                         is_start=is_start, is_stop=is_stop: e.matmul(
                                    ps_pv[0:65, hh * 512:(hh + 1) * 512],
                                    vv[:, lh * 2080 + j * 65: lh * 2080 + (j + 1) * 65],
                                    pt[:, hh * 3072 + slot * 1536 + ji * 512:
                                       hh * 3072 + slot * 1536 + (ji + 1) * 512],
                                    start=is_start, stop=is_stop,
                                    skip_group_check=True),
                                    inc="P" if ji == len(wins[w]) - 1 else None,
                                    waits=[tok_expw[hh][slot],
                                           pv[hh].wgate() if is_start else None]
                                    if ji == 0 else [])
                                if is_stop:
                                    tok_pv_stop[hh] = t_pv
                            tok_pv[hh][slot] = t_pv
                            b_kv.read(t_pv)
                            return t_pv

                        for w in range(nW):
                            for hh in (0, 1):
                                t_qk = emit_qk(w, hh)
                                emit_exp(w, hh, t_qk)
                            if w > 0:
                                for hh in (0, 1):
                                    emit_pv(w - 1, hh)
                        for hh in (0, 1):
                            emit_pv(nW - 1, hh)
                        for hh in (0, 1):
                            lh = 2 * pp + hh
                            h = g * GH + lh
                            pv[hh].wrote(tok_pv_stop[hh])
                            t_re = E("vector", lambda e, hh=hh, pp=pp: e.tensor_copy(
                                rawstage[hh * 64:(hh + 1) * 64, pp * 512:(pp + 1) * 512],
                                ps_pv[0:64, hh * 512:(hh + 1) * 512]),
                                inc="V",
                                waits=[tok_pv_stop[hh],
                                       b_raw.wgate() if (pp == 0 and hh == 0) else None])
                            t_le = E("vector", lambda e, hh=hh: e.tensor_copy(
                                lrow[:, 0:512],
                                ps_pv[64:65, hh * 512:(hh + 1) * 512]),
                                inc="V", waits=[b_lrow[0].wgate()])
                            b_lrow[0].wrote(t_le)
                            t_ld = E("sync", lambda e, h=h, ti=a_ti: e.dma_start(
                                Ls[h:h + 1, ti * 512:(ti + 1) * 512],
                                lrow[:, 0:512]),
                                inc="S", waits=[t_le])
                            b_lrow[0].read(t_ld)
                            pv[hh].read(t_re)
                            pv[hh].read(t_le)
                            raw_toks.append(t_re)
                        if ph == 0 and mid_emit is not None:
                            mid_emit()
                    t_rd = E("sync", lambda e, ti=a_ti, g=g: e.dma_start(
                        rawTs.rearrange("(gg pp half hd) tt -> gg (half hd) pp tt",
                                        gg=2, pp=4, half=2)[g][:, :, ti * 512:(ti + 1) * 512],
                        rawstage[:, :].rearrange("p (pp w) -> p pp w", w=512)),
                        inc="S", waits=[raw_toks])
                    b_raw.wrote(t_rd)
                    return t_rd

                def emit_t1(ti, tok_x, tok_st):
                    sp_toks = []
                    for u in range(4):
                        t_ap = ln_apply(u, xt, tok_x, tok_st)
                        b_xt.read(t_ap)
                        t_sp1 = E("sync", lambda e, ti=ti, u=u: e.dma_start(
                            t1s[ti * QT + u * 128: ti * QT + (u + 1) * 128, :],
                            lntok[:, (u % 2) * 1024:(u % 2 + 1) * 1024]),
                            inc="S", waits=[t_ap])
                        b_lnslot[u % 2].read(t_sp1)
                        sp_toks.append(t_sp1)
                    t_tl = None
                    for c in range(8):
                        t_tl = E("sync", lambda e, ti=ti, c=c: e.dma_start(
                            tft[:, c * 512:(c + 1) * 512],
                            t1s[ti * QT:(ti + 1) * QT, c * 128:(c + 1) * 128],
                            transpose=True),
                            inc="S",
                            waits=[sp_toks, b_tft.wgate()] if c == 0 else [])
                    b_tft.wrote(t_tl)
                    return t_tl

                pend_x = None
                pend_st = None
                pend_tft = None
                pend_att = None
                tok_tile_end = None
                for ti in range(NT):
                    if g == 0:
                        if ti == 0:
                            tok_x = load_x(0, tok_start)
                            tok_st = ln1_stats(0, tok_x)
                            tok_tft = emit_t1(0, tok_x, tok_st)
                        else:
                            tok_x, tok_tft = pend_x, pend_tft
                        tok_y1t = None
                    else:
                        if ti == 0:
                            t_y = None
                            for c in range(8):
                                t_y = E("sync", lambda e, ti=ti, c=c: e.dma_start(
                                    y1t[:, c * 512:(c + 1) * 512],
                                    y1s[ti * QT:(ti + 1) * QT, c * 128:(c + 1) * 128],
                                    transpose=True),
                                    inc="S",
                                    waits=[b_y1t.wgate(), tok_start] if c == 0 else [])
                            tok_y1t = t_y
                            b_y1t.wrote(t_y)
                        else:
                            tok_y1t = pend_x
                            b_y1t.wrote(tok_y1t)

                    if g == 0:
                        t_x1, s2 = ffn_block([tok_tft], x1t, b_x1t,
                                             xt, b_xt, tok_x, [0, 1, 2, 3])
                        t_x1d = E("sync", lambda e, ti=ti: e.dma_start(
                            x1s.rearrange("(tt u p) d -> tt p u d", u=4, p=128)[ti],
                            x1t[:, :].rearrange("p (u d) -> p u d", u=4)),
                            inc="S", waits=[t_x1])
                        b_x1t.read(t_x1d)
                        sq2 = []
                        for u in range(4):
                            t_sq = E("scalar", lambda e, u=u: e.activation(
                                xt[:, u * 1024:(u + 1) * 1024],
                                x1t[:, u * 1024:(u + 1) * 1024],
                                AF.Square, accum_out=stats[:, 4 + u:5 + u]),
                                inc="A", waits=[s2[u],
                                                b_stats.wgate() if u == 0 else None])
                            b_x1t.read(t_sq)
                            b_xt.read(t_sq)
                            sq2.append(t_sq)
                        b_stats.wrote(sq2[3])
                        tok_st2 = ln_stats_chain(s2[3], sq2[3])

                        hold = {}

                        def mid(ti=ti, t_x1=t_x1, tok_st2=tok_st2, hold=hold):
                            sp_toks = []
                            for u in range(4):
                                t_ap = ln_apply(u, x1t, t_x1, tok_st2)
                                b_x1t.read(t_ap)
                                t_spill = E("sync", lambda e, ti=ti, u=u: e.dma_start(
                                    y1s[ti * QT + u * 128: ti * QT + (u + 1) * 128, :],
                                    lntok[:, (u % 2) * 1024:(u % 2 + 1) * 1024]),
                                    inc="S", waits=[t_ap])
                                b_lnslot[u % 2].read(t_spill)
                                sp_toks.append(t_spill)
                            t_yl = None
                            for c in range(8):
                                t_yl = E("sync", lambda e, ti=ti, c=c: e.dma_start(
                                    y1t[:, c * 512:(c + 1) * 512],
                                    y1s[ti * QT:(ti + 1) * QT, c * 128:(c + 1) * 128],
                                    transpose=True),
                                    inc="S",
                                    waits=[sp_toks, b_y1t.wgate()] if c == 0 else [])
                            hold["y1t"] = t_yl

                        if pend_att is not None:
                            tok_tile_end = emit_attention(*pend_att, mid_emit=mid)
                        else:
                            mid()
                        tok_y1t = hold["y1t"]
                        b_y1t.wrote(tok_y1t)
                    else:
                        if pend_att is not None:
                            tok_tile_end = emit_attention(*pend_att)

                    qt_toks = []
                    kv_toks = []
                    for pp in range(4):
                        hA = g * GH + 2 * pp
                        qb = pp % 2
                        t_mq = None
                        for c in range(8):
                            t_mq = E("tensor", lambda e, c=c, hA=hA, qb=qb: e.matmul(
                                ps_sc[:, qb * 512:(qb + 1) * 512],
                                wqkvT[:, c * 3072 + hA * 64: c * 3072 + hA * 64 + 128],
                                y1t[:, c * 512:(c + 1) * 512],
                                start=(c == 0), stop=(c == 7), skip_group_check=True),
                                inc="P" if c == 7 else None,
                                waits=[tok_y1t, sc[qb].wgate(),
                                       tok_init] if c == 0 else [])
                        sc[qb].wrote(t_mq)
                        b_y1t.read(t_mq)
                        t_qe = E("vector", lambda e, pp=pp, qb=qb: e.tensor_scalar(
                            qt[:, pp * 512:(pp + 1) * 512],
                            ps_sc[:, qb * 512:(qb + 1) * 512], 0.125, None,
                            op0=ALU.mult), inc="V",
                            waits=[t_mq, b_qt.wgate() if pp == 0 else None])
                        t_qs = E("vector", lambda e, pp=pp: e.tensor_tensor(
                            qsqbf[:, :], qt[:, pp * 512:(pp + 1) * 512],
                            qt[:, pp * 512:(pp + 1) * 512], op=ALU.mult),
                            inc="V", waits=[t_qe, b_qsq.wgate()])
                        sc[qb].read(t_qe)
                        sc[qb].read(t_qs)
                        b_qsq.wrote(t_qs)
                        qt_toks.append(t_qe)
                        t_eq = E("tensor", lambda e: e.matmul(
                            ps_pv[:, 0:512], eab_sb[:, :], qsqbf[:, :],
                            start=True, stop=True, skip_group_check=True),
                            inc="P", waits=[t_qs, pv[0].wgate()])
                        pv[0].wrote(t_eq)
                        b_qsq.read(t_eq)
                        E("vector", lambda e, pp=pp: e.reduce_max(
                            qmaxsc[0:1, pp:pp + 1], ps_pv[0:1, 0:512], axis=AX.X),
                            waits=[t_eq, b_cmax.wgate() if pp == 0 else None])
                        t_r2 = E("vector", lambda e, pp=pp: e.reduce_max(
                            qmaxsc[64:65, pp:pp + 1], ps_pv[64:65, 0:512], axis=AX.X),
                            inc="V")
                        pv[0].read(t_r2)
                        kb = 2 + pp % 2
                        t_mk2 = None
                        for c in range(8):
                            t_mk2 = E("tensor", lambda e, c=c, hA=hA, kb=kb: e.matmul(
                                ps_sc[:, kb * 512:(kb + 1) * 512],
                                wqkvT[:, c * 3072 + 1024 + hA * 64:
                                      c * 3072 + 1024 + hA * 64 + 128],
                                y1t[:, c * 512:(c + 1) * 512],
                                start=(c == 0), stop=(c == 7), skip_group_check=True),
                                inc="P" if c == 7 else None,
                                waits=[tok_y1t, sc[kb].wgate(),
                                       b_kv.w if (ti == 0 and pp == 0) else None]
                                if c == 0 else [])
                        sc[kb].wrote(t_mk2)
                        b_y1t.read(t_mk2)
                        t_ke = E("vector", lambda e, pp=pp, ti=ti, kb=kb: e.tensor_copy(
                            kT[:, pp * T + ti * 512: pp * T + (ti + 1) * 512],
                            ps_sc[:, kb * 512:(kb + 1) * 512]),
                            inc="V", waits=[t_mk2])
                        t_ks = E("vector", lambda e, pp=pp, ti=ti: e.tensor_tensor(
                            qsqbf[:, :], kT[:, pp * T + ti * 512: pp * T + (ti + 1) * 512],
                            kT[:, pp * T + ti * 512: pp * T + (ti + 1) * 512],
                            op=ALU.mult),
                            inc="V", waits=[t_ke, b_qsq.wgate()])
                        sc[kb].read(t_ke)
                        sc[kb].read(t_ks)
                        b_qsq.wrote(t_ks)
                        kv_toks.append(t_ke)
                        t_ek = E("tensor", lambda e: e.matmul(
                            ps_pv[:, 512:1024], eab_sb[:, :], qsqbf[:, :],
                            start=True, stop=True, skip_group_check=True),
                            inc="P", waits=[t_ks, pv[1].wgate()])
                        pv[1].wrote(t_ek)
                        b_qsq.read(t_ek)
                        E("vector", lambda e, pp=pp: e.reduce_max(
                            kmaxsc[0:1, pp:pp + 1], ps_pv[0:1, 512:1024], axis=AX.X),
                            waits=[t_ek])
                        t_r4 = E("vector", lambda e, pp=pp: e.reduce_max(
                            kmaxsc[64:65, pp:pp + 1], ps_pv[64:65, 512:1024],
                            axis=AX.X), inc="V")
                        pv[1].read(t_r4)
                        b_cmax.read(t_r4)
                    b_qt.wrote(qt_toks[-1])
                    for u in range(4):
                        j = ti * 4 + u
                        vb = 4 + u % 2
                        t_mv = None
                        for c in range(8):
                            t_mv = E("tensor", lambda e, c=c, u=u, g=g, vb=vb: e.matmul(
                                ps_sc[:, vb * 512:(vb + 1) * 512],
                                y1t[:, c * 512 + u * 128: c * 512 + (u + 1) * 128],
                                wqkvT[:, c * 3072 + 2048 + g * 512:
                                      c * 3072 + 2048 + (g + 1) * 512],
                                start=(c == 0), stop=(c == 7), skip_group_check=True),
                                inc="P" if c == 7 else None,
                                waits=[tok_y1t, sc[vb].wgate()] if c == 0 else [])
                        sc[vb].wrote(t_mv)
                        b_y1t.read(t_mv)
                        t_ve = E("vector", lambda e, j=j, vb=vb: e.tensor_copy(
                            vv[:, :].rearrange("p (lh w) -> p lh w", w=32 * 65)
                              [:, :, j * 65:j * 65 + 64],
                            ps_sc[:, vb * 512:(vb + 1) * 512]
                                .rearrange("p (lh w) -> p lh w", w=64)),
                            inc="V",
                            waits=[t_mv, b_kv.w if (ti == 0 and u == 0) else None])
                        sc[vb].read(t_ve)
                        kv_toks.append(t_ve)

                    t_mku = E("vector", lambda e: e.tensor_max(
                        mk2[:, :], mk2[:, :], kmaxsc[:, :]), inc="V",
                        waits=[t_r4, t_mk])
                    t_nc0 = E("vector", lambda e: e.tensor_tensor(
                        negc[:, :], qmaxsc[:, :], mk2[:, :], op=ALU.mult),
                        inc="V", waits=[t_mku, t_r2])
                    t_nc1 = E("scalar", lambda e: e.activation(
                        negc[:, :], negc[:, :], AF.Ln), inc="A", waits=[t_nc0])
                    t_nc2 = E("scalar", lambda e: e.activation(
                        negc[:, :], negc[:, :], AF.Exp, scale=0.5,
                        bias=constc[:, 0:1]), inc="A", waits=[t_constc, t_nc1])
                    t_nc3 = E("vector", lambda e: e.tensor_scalar_mul(
                        negc[:, :], negc[:, :], -1.0), inc="V", waits=[t_nc2])
                    b_cmax.wrote(t_nc3)
                    t_cbm = None
                    for lh in range(GH):
                        pp, half = lh // 2, lh % 2
                        t_cbm = E("tensor", lambda e, lh=lh, pp=pp, half=half: e.matmul(
                            ps_pv[:, lh:lh + 1],
                            ones_f[half * 64:half * 64 + 1, :],
                            negc[half * 64:half * 64 + 1, pp:pp + 1],
                            start=True, stop=True, skip_group_check=True),
                            inc="P" if lh == GH - 1 else None,
                            waits=[t_nc3, pv[0].wgate()] if lh == 0 else [])
                    pv[0].wrote(t_cbm)
                    b_cmax.read(t_cbm)
                    t_cb = E("vector", lambda e: e.tensor_copy(
                        cb[:, 0:8], ps_pv[:, 0:8]), inc="V",
                        waits=[t_cbm, b_cb.wgate()])
                    pv[0].read(t_cb)
                    b_cb.wrote(t_cb)

                    if ti + 1 < NT:
                        if g == 0:
                            pend_x = load_x(ti + 1, None)
                            pend_st = ln1_stats(ti + 1, pend_x)
                            pend_tft = emit_t1(ti + 1, pend_x, pend_st)
                        else:
                            t_y = None
                            for c in range(8):
                                t_y = E("sync", lambda e, ti=ti, c=c: e.dma_start(
                                    y1t[:, c * 512:(c + 1) * 512],
                                    y1s[(ti + 1) * QT:(ti + 2) * QT,
                                        c * 128:(c + 1) * 128],
                                    transpose=True),
                                    inc="S",
                                    waits=[b_y1t.wgate()] if c == 0 else [])
                            pend_x = t_y
                    pend_att = (ti, kv_toks, qt_toks, t_cb)
                tok_tile_end = emit_attention(*pend_att)
                return tok_tile_end

            tok_A = group_pass(0, [tok_init, tok_masks])
            tok_Bp = group_pass(1, [tok_A])
            tok_AB = [tok_Bp, ("P", plan.cnt["P"]), ("A", plan.cnt["A"]),
                      ("V", plan.cnt["V"]), ("S", plan.cnt["S"])]

        cstk = contextlib.ExitStack()
        with cstk:
            woutT = sbuf("woutT", [128, 8 * 1024], BF16, cstk)
            rawin = sbuf("rawin", [128, 8 * 512], BF16, cstk)
            scaledT = sbuf("scaledT", [128, 8 * 512], BF16, cstk)
            epat_sb = sbuf("epat_sb", [16, 8 * 128], F32, cstk)
            hpat_sb = sbuf("hpat_sb", [16, 128], F32, cstk)
            Lm = sbuf("Lm", [128, 512], F32, cstk)
            rcl = sbuf("rcl", [128, 512], F32, cstk)
            S128 = sbuf("S128", [128, 512], F32, cstk)
            Smat = sbuf("Smat", [16, T], F32, cstk)
            gA = sbuf("gA", [128, 512], F32, cstk)
            gB = sbuf("gB", [128, 512], F32, cstk)
            gC = sbuf("gC", [128, 512], F32, cstk)
            gD = sbuf("gD", [128, 512], F32, cstk)
            srow = sbuf("srow", [1, 128], F32, cstk)
            rs16 = sbuf("rs16", [16, 1], F32, cstk)
            xt2 = sbuf("xt2", [128, 4 * 1024], F32, cstk)
            lnf32 = sbuf("lnf32", [128, 2 * 1024], F32, cstk)

            b_rawin, b_scaled = Buf(), Buf()
            b_lnf = [Buf(), Buf()]

            t_ep = E("sync", lambda e: e.dma_start(epat_sb[:, :], c_epat),
                     inc="S", waits=[tok_AB])
            tok_woutT = None
            for c in range(8):
                tok_woutT = E("sync", lambda e, c=c: e.dma_start(
                    woutT[:, c * 1024:(c + 1) * 1024],
                    wout_bf[:, c * 128:(c + 1) * 128], transpose=True),
                    inc="S", waits=[t_wo, tok_AB] if c == 0 else [])
            t_lm = E("sync", lambda e: e.dma_start(
                Lm[:, :], Ls.rearrange("h (ee w) -> (h ee) w", ee=8)),
                inc="S", waits=[tok_AB])
            t_hp = E("sync", lambda e: e.dma_start(hpat_sb[:, :], c_hpat),
                     inc="S", waits=[tok_AB])

            t_sv = None
            for i in range(3):
                t_sv = E("tensor", lambda e, i=i: e.matmul(
                    ps_pv[:, 8 + i:9 + i], ones_f[0:1, :], svals[:, i:i + 1],
                    start=True, stop=True, skip_group_check=True),
                    inc="P" if i == 2 else None,
                    waits=[t_svals, pv[0].wgate(), tok_AB] if i == 0 else [])
            pv[0].wrote(t_sv)
            t_svc = E("vector", lambda e: e.tensor_copy(
                smallf[:, 0:3], ps_pv[:, 8:11]), inc="V", waits=[t_sv])
            pv[0].read(t_svc)
            acol, bcol, dtcol = smallf[:, 0:1], smallf[:, 1:2], smallf[:, 2:3]
            k1 = E("vector", lambda e: e.tensor_scalar(
                smallf[:, 3:4], dtcol, 1.0 / 12.5, None, op0=ALU.mult),
                inc="V", waits=[t_svc])
            k2 = E("vector", lambda e: e.tensor_tensor(
                smallf[:, 4:5], smallf[:, 3:4], bcol, op=ALU.mult),
                inc="V", waits=[k1])
            k3 = E("vector", lambda e: e.tensor_scalar(
                smallf[:, 4:5], smallf[:, 4:5], 1.0, None, op0=ALU.add),
                inc="V", waits=[k2])
            k4 = E("vector", lambda e: e.reciprocal(smallf[:, 5:6], smallf[:, 4:5]),
                inc="V", waits=[k3])
            k5 = E("vector", lambda e: e.tensor_tensor(
                smallf[:, 6:7], smallf[:, 3:4], smallf[:, 5:6], op=ALU.mult),
                inc="V", waits=[k4])
            k6 = E("vector", lambda e: e.tensor_tensor(
                smallf[:, 9:10], acol, smallf[:, 6:7], op=ALU.mult),
                inc="V", waits=[k5])
            t_kd = E("vector", lambda e: e.tensor_scalar(
                smallf[:, 10:11], smallf[:, 6:7], -1.0, None, op0=ALU.mult),
                inc="V", waits=[k6])
            akd = smallf[:, 9:10]
            negkd = smallf[:, 10:11]
            dtc = smallf[:, 2:3]

            t_rl = E("vector", lambda e: e.reciprocal(rcl[:, :], Lm[:, :]),
                     inc="V", waits=[t_lm])
            tEa = E("vector", lambda e: e.tensor_tensor(
                gA[:, :], Lm[:, :], rcl[:, :], op=ALU.mult), inc="V",
                waits=[t_rl])
            m1 = E("vector", lambda e: e.reduce_max(
                smallf[:, 11:12], gA[:, :], axis=AX.X,
                apply_absolute_value=True), inc="V", waits=[tEa])
            t_t1 = E("tensor", lambda e: e.transpose(
                ps_pv[0:1, 16:144], smallf[:, 11:12], eyef_sb[:, :]),
                inc="P", waits=[m1, pv[0].wgate()])
            pv[0].wrote(t_t1)
            t_sr = E("vector", lambda e: e.tensor_copy(srow[:, :], ps_pv[0:1, 16:144]),
                     inc="V", waits=[t_t1])
            pv[0].read(t_sr)
            def v8(e_idx):
                return srow[:, :].rearrange("o (h ee) -> o h ee", ee=8)[:, :, e_idx]
            t_mx = t_sr
            for ee in range(1, 8):
                t_mx = E("vector", lambda e, ee=ee: e.tensor_tensor(
                    v8(0), v8(0), v8(ee), op=ALU.max),
                    inc="V", waits=[t_mx])
            t_mc = E("vector", lambda e: e.tensor_scalar(
                v8(0), v8(0), 1e-6, None, op0=ALU.max), inc="V", waits=[t_mx])
            t_cp = E("vector", lambda e: e.tensor_copy(
                srow[:, 0:16], v8(0)), inc="V", waits=[t_mc])
            t_rc = E("vector", lambda e: e.reciprocal(srow[:, 0:16], srow[:, 0:16]),
                     inc="V", waits=[t_cp])
            t_t2 = E("tensor", lambda e: e.transpose(
                ps_pv[0:16, 0:1], srow[:, 0:16], eyef_sb[0:1, 0:1]),
                inc="P", waits=[t_rc, pv[0].wgate()])
            pv[0].wrote(t_t2)
            t_r16 = E("vector", lambda e: e.tensor_copy(rs16[:, :], ps_pv[0:16, 0:1]),
                      inc="V", waits=[t_t2])
            pv[0].read(t_r16)
            t_bc = E("tensor", lambda e: e.matmul(
                ps_pv[:, 4:5], hpat_sb[:, :], rs16[:, :],
                start=True, stop=True, skip_group_check=True),
                inc="P", waits=[t_r16, t_hp, pv[0].wgate()])
            pv[0].wrote(t_bc)
            t_rs128 = E("vector", lambda e: e.tensor_copy(
                smallf[:, 8:9], ps_pv[:, 4:5]), inc="V", waits=[t_bc])
            pv[0].read(t_rs128)
            rscale = smallf[:, 8:9]

            t_tg = E("scalar", lambda e: e.activation(
                gB[:, :], gA[:, :], AF.Sigmoid, scale=10.0,
                bias=constc[:, 1:2]), inc="A", waits=[tEa, t_constc])
            E("vector", lambda e: e.tensor_scalar(
                gB[:, :], gB[:, :], 0.9, 0.1, op0=ALU.mult, op1=ALU.add),
                waits=[t_tg])
            E("vector", lambda e: e.scalar_tensor_tensor(
                gB[:, :], gA[:, :], rscale, gB[:, :],
                op0=ALU.mult, op1=ALU.mult), waits=[t_rs128])
            E("vector", lambda e: e.tensor_scalar(
                gC[:, :], gB[:, :], dtc, None, op0=ALU.mult), waits=[t_kd])
            E("vector", lambda e: e.tensor_scalar(
                gB[:, :], gB[:, :], akd, None, op0=ALU.subtract))
            E("vector", lambda e: e.scalar_tensor_tensor(
                gB[:, :], gC[:, :], negkd, gB[:, :], op0=ALU.mult, op1=ALU.add))
            E("vector", lambda e: e.scalar_tensor_tensor(
                gB[:, :], gC[:, :], 1.0, gB[:, :], op0=ALU.mult, op1=ALU.add))
            E("vector", lambda e: e.tensor_tensor(
                gD[:, :], gC[:, :], gC[:, :], op=ALU.mult))
            E("vector", lambda e: e.tensor_tensor(
                gD[:, :], gD[:, :], gC[:, :], op=ALU.mult))
            E("vector", lambda e: e.scalar_tensor_tensor(
                gB[:, :], gD[:, :], -1.0 / 3.0, gB[:, :],
                op0=ALU.mult, op1=ALU.add))
            t_v2 = E("vector", lambda e: e.scalar_tensor_tensor(
                gC[:, :], gB[:, :], dtc, gC[:, :],
                op0=ALU.mult, op1=ALU.add), inc="V")
            t_sg = E("scalar", lambda e: e.activation(
                gB[:, :], gC[:, :], AF.Sigmoid), inc="A", waits=[t_v2])
            E("vector", lambda e: e.tensor_scalar(
                gB[:, :], gB[:, :], 0.5, 0.5, op0=ALU.mult, op1=ALU.add),
                waits=[t_sg])
            E("vector", lambda e: e.tensor_tensor(
                gD[:, :], gB[:, :], gA[:, :], op=ALU.mult))
            E("vector", lambda e: e.tensor_scalar(
                gD[:, :], gD[:, :], 1e-8, None, op0=ALU.add))
            E("vector", lambda e: e.reciprocal(gD[:, :], gD[:, :]))
            E("vector", lambda e: e.tensor_tensor(
                gB[:, :], gB[:, :], gD[:, :], op=ALU.mult))
            t_s128 = E("vector", lambda e: e.tensor_tensor(
                S128[:, :], gB[:, :], rcl[:, :], op=ALU.mult), inc="V")
            t_sd = E("sync", lambda e: e.dma_start(S_hbm, S128[:, :]),
                     inc="S", waits=[t_s128])
            tok_S = E("sync", lambda e: e.dma_start(
                Smat[:, :], S_hbm.rearrange("(h ee) w -> h (ee w)", ee=8)),
                inc="S", waits=[t_sd])

            xbufs = [xt, xt2]
            xbt = [b_xt, Buf()]
            b_x2 = b_x1t

            def load_c(ti, first):
                bi = ti % 2
                tok = E("sync", lambda e, ti=ti, bi=bi: e.dma_start(
                    xbufs[bi][:, :].rearrange("p (u d) -> p u d", u=4),
                    x1s.rearrange("(tt u p) d -> tt p u d", u=4, p=128)[ti]),
                    inc="S", waits=[xbt[bi].wgate(), tok_AB if first else None])
                xbt[bi].wrote(tok)
                return tok

            sc_toks = {}

            def stage_scale(ti, first=False):
                tok_ri = E("sync", lambda e, ti=ti: e.dma_start(
                    rawin[:, :].rearrange("p (c w) -> p c w", w=512),
                    rawTs.rearrange("(c p) tt -> p c tt", p=128)[:, :, ti * 512:(ti + 1) * 512]),
                    inc="S", waits=[b_rawin.wgate(), tok_AB if first else None])
                b_rawin.wrote(tok_ri)
                t_last = None
                for c in range(8):
                    bank = c % 4
                    t_bc2 = E("tensor", lambda e, c=c, ti=ti, bank=bank: e.matmul(
                        ps_sc[:, bank * 512:(bank + 1) * 512],
                        epat_sb[:, c * 128:(c + 1) * 128],
                        Smat[:, ti * 512:(ti + 1) * 512],
                        start=True, stop=True, skip_group_check=True),
                        inc="P",
                        waits=[tok_S, t_ep, sc[bank].wgate(),
                               tok_woutT if first and c == 0 else None])
                    sc[bank].wrote(t_bc2)
                    t_mul = E("vector", lambda e, c=c, bank=bank: e.tensor_tensor(
                        scaledT[:, c * 512:(c + 1) * 512],
                        rawin[:, c * 512:(c + 1) * 512],
                        ps_sc[:, bank * 512:(bank + 1) * 512], op=ALU.mult),
                        inc="V",
                        waits=[t_bc2, tok_ri, b_scaled.wgate() if c == 0 else None])
                    sc[bank].read(t_mul)
                    b_rawin.read(t_mul)
                    t_last = t_mul
                b_scaled.wrote(t_last)
                sc_toks[ti] = t_last

            tok_x1 = load_c(0, True)
            stage_scale(0, True)
            tok_out_final = None
            for ti in range(NT):
                bi = ti % 2
                cur = xbufs[bi]
                curb = xbt[bi]
                t_sc_last = sc_toks[ti]
                s_toks = []
                for u in range(4):
                    for half in range(2):
                        pvs = half
                        t_mo = None
                        for c in range(8):
                            t_mo = E("tensor", lambda e, c=c, u=u, half=half, pvs=pvs: e.matmul(
                                ps_pv[:, pvs * 512:(pvs + 1) * 512],
                                scaledT[:, c * 512 + u * 128: c * 512 + (u + 1) * 128],
                                woutT[:, c * 1024 + half * 512: c * 1024 + (half + 1) * 512],
                                start=(c == 0), stop=(c == 7), skip_group_check=True),
                                inc="P" if c == 7 else None,
                                waits=[t_sc_last, pv[pvs].wgate()] if c == 0 else [])
                        pv[pvs].wrote(t_mo)
                        b_scaled.read(t_mo)
                        col = u if half == 0 else 12 + u
                        tgt = stats if half == 0 else smallf
                        t_x2 = E("vector", lambda e, u=u, half=half, pvs=pvs,
                                 col=col, tgt=tgt, cur=cur: e.scalar_tensor_tensor(
                            x1t[:, u * 1024 + half * 512: u * 1024 + (half + 1) * 512],
                            ps_pv[:, pvs * 512:(pvs + 1) * 512], 1.0,
                            cur[:, u * 1024 + half * 512: u * 1024 + (half + 1) * 512],
                            op0=ALU.mult, op1=ALU.add,
                            accum_out=tgt[:, col:col + 1]),
                            inc="V",
                            waits=[t_mo, tok_x1,
                                   b_x2.wgate() if (u == 0 and half == 0) else None,
                                   b_stats.wgate() if (u == 0 and half == 0) else None])
                        pv[pvs].read(t_x2)
                        curb.read(t_x2)
                        s_toks.append(t_x2)
                b_x2.wrote(s_toks[-1])
                if ti + 1 < NT:
                    tok_x1 = load_c(ti + 1, False)
                    stage_scale(ti + 1)
                t_cmb = E("vector", lambda e: e.tensor_add(
                    stats[:, 0:4], stats[:, 0:4], smallf[:, 12:16]),
                    inc="V", waits=[s_toks])
                sq3 = []
                for u in range(4):
                    t_sq = E("scalar", lambda e, u=u, cur=cur: e.activation(
                        cur[:, u * 1024:(u + 1) * 1024],
                        x1t[:, u * 1024:(u + 1) * 1024],
                        AF.Square, accum_out=stats[:, 4 + u:5 + u]),
                        inc="A", waits=[s_toks[2 * u + 1],
                                        b_stats.wgate() if u == 0 else None])
                    b_x2.read(t_sq)
                    curb.read(t_sq)
                    sq3.append(t_sq)
                b_stats.wrote(sq3[3])
                tok_st3 = ln_stats_chain(t_cmb, sq3[3])
                tp3 = []
                for u in range(4):
                    slot = u % 2
                    t_ap = E("vector", lambda e, u=u, slot=slot: e.tensor_scalar(
                        lnf32[:, slot * 1024:(slot + 1) * 1024],
                        x1t[:, u * 1024:(u + 1) * 1024],
                        stats[:, 24 + u:25 + u], stats[:, 28 + u:29 + u],
                        op0=ALU.mult, op1=ALU.add),
                        inc="V",
                        waits=[s_toks[-1], tok_st3, b_lnf[slot].wgate()])
                    b_lnf[slot].wrote(t_ap)
                    b_x2.read(t_ap)
                    tps = transpose_subtile(u, tft, b_tft, t_ap, lnf32)
                    for t_ev in tps:
                        b_lnf[slot].read(t_ev)
                    tp3 += tps
                b_tft.wrote(tp3[-1])
                t_x3, _s = ffn_block([tp3, b_tft.w], cur, curb, x1t, b_x2,
                                     s_toks[-1], [8, 9, 10, 11])
                tok_out_final = E("sync", lambda e, ti=ti, cur=cur: e.dma_start(
                    out.rearrange("(tt u p) d -> tt p u d", u=4, p=128)[ti],
                    cur[:, :].rearrange("p (u d) -> p u d", u=4)),
                    inc="S", waits=[t_x3])
                curb.read(tok_out_final)

            E("sync", lambda e: e.nop(), waits=[tok_out_final])

        plan.finish(ctx.enter_context(nc.Block()))
    return nc


_CONSTS = None
_NC_CACHE = {}


def kernel(L0_attn_x, L0_attn_Wqkv, L0_attn_Wout, L0_attn_a, L0_attn_b,
           L0_attn_dt, L0_ffn_Win, L0_ffn_bin_, L0_ffn_W1, L0_ffn_b1,
           L0_ffn_W2, L0_ffn_b2, L0_ffn_Wout, L0_ffn_bout):
    global _CONSTS
    if _CONSTS is None:
        _CONSTS = build_consts()
    consts = _CONSTS

    xf = np.ascontiguousarray(np.asarray(L0_attn_x, dtype=np.float32))
    common = {
        "wqkv": np.ascontiguousarray(np.asarray(L0_attn_Wqkv, dtype=np.float32)),
        "wout": np.ascontiguousarray(np.asarray(L0_attn_Wout, dtype=np.float32)),
        "sa": np.asarray(L0_attn_a, dtype=np.float32).reshape(1),
        "sb": np.asarray(L0_attn_b, dtype=np.float32).reshape(1),
        "sdt": np.asarray(L0_attn_dt, dtype=np.float32).reshape(1),
        "fwin": np.ascontiguousarray(np.asarray(L0_ffn_Win, dtype=np.float32)),
        "fbin": np.asarray(L0_ffn_bin_, dtype=np.float32).reshape(M),
        "fw1": np.asarray(L0_ffn_W1, dtype=np.float32).reshape(M * INNER),
        "fb1": np.asarray(L0_ffn_b1, dtype=np.float32).reshape(M * INNER),
        "fw2": np.asarray(L0_ffn_W2, dtype=np.float32).reshape(M * INNER),
        "fb2": np.asarray(L0_ffn_b2, dtype=np.float32).reshape(M),
        "fwout": np.ascontiguousarray(np.asarray(L0_ffn_Wout, dtype=np.float32)),
        "fbout": np.asarray(L0_ffn_bout, dtype=np.float32).reshape(D),
    }
    common.update(consts)

    nc = build_program(debug=False)
    in_maps = []
    for core in range(8):
        m = dict(common)
        m["x"] = np.ascontiguousarray(xf[core])
        in_maps.append(m)
    res = run_bass_kernel_spmd(nc, in_maps, core_ids=list(range(8)))
    return np.stack([np.asarray(r["out"]) for r in res.results], axis=0)
